# version 27
# baseline (speedup 1.0000x reference)
import numpy as np
import ml_dtypes
import concourse.bass as bass
import concourse.mybir as mybir
from concourse.bass_utils import run_bass_kernel_spmd
from contextlib import ExitStack

F32 = mybir.dt.float32
BF16 = mybir.dt.bfloat16
I32 = mybir.dt.int32
AF = mybir.ActivationFunctionType
ALU = mybir.AluOpType
AX = mybir.AxisListType

NBLK = 16
D = 1024
HC = 24
SCALE = 512.0 ** -0.5
LNSCALE = float(np.log(SCALE * 1.25))
EPS = 1e-5
NEG = -1e30
QDEQ = 1.0 / 127.0
BF = ml_dtypes.bfloat16


class Sem:
    def __init__(self, es, nc, name):
        self.h = es.enter_context(nc.semaphore(name))
        self.n = 0


class EngQ:
    def __init__(self):
        self.ops = []

    def run(self, eng):
        for op in self.ops:
            op(eng)


def build_nc():
    nc = bass.Bass("TRN2", target_bir_lowering=False, debug=False)

    xc = nc.dram_tensor("xc", [8192, D], F32, kind="ExternalInput").ap()
    wqkvT = nc.dram_tensor("wqkvT", [D, 1536], F32, kind="ExternalInput").ap()
    woutT = nc.dram_tensor("woutT", [512, D], F32, kind="ExternalInput").ap()
    boutb = nc.dram_tensor("boutb", [128, D], F32, kind="ExternalInput").ap()
    wgT = nc.dram_tensor("wgT", [HC, 128, 1024], I32, kind="ExternalInput").ap()
    wuT = nc.dram_tensor("wuT", [HC, 128, 1024], I32, kind="ExternalInput").ap()
    wdT = nc.dram_tensor("wdT", [HC, 128, 1024], I32, kind="ExternalInput").ap()
    mask2d = nc.dram_tensor("mask2d", [128, 256], BF16, kind="ExternalInput").ap()
    mask3d = nc.dram_tensor("mask3d", [128, 256], BF16, kind="ExternalInput").ap()
    pc01d = nc.dram_tensor("pc01d", [128, 1], F32, kind="ExternalInput").ap()
    identd = nc.dram_tensor("identd", [128, 128], BF16, kind="ExternalInput").ap()
    onescd = nc.dram_tensor("onescd", [128, 1], BF16, kind="ExternalInput").ap()
    onesrd = nc.dram_tensor("onesrd", [1, 128], F32, kind="ExternalInput").ap()
    onefd = nc.dram_tensor("onefd", [1, 1], F32, kind="ExternalInput").ap()
    econstd = nc.dram_tensor("econstd", [128, 2], F32, kind="ExternalInput").ap()
    out_c = nc.dram_tensor("out_c", [4096, D], F32, kind="ExternalOutput").ap()
    x1s = nc.dram_tensor("x1s", [4096, D], F32, kind="Internal").ap()
    xnfms = nc.dram_tensor("xnfms", [8, 128, 4096], BF16, kind="Internal").ap()

    es = ExitStack()
    S = lambda name: Sem(es, nc, name)
    sX = S("sX"); sXr = S("sXr"); sW = S("sW"); sCast = S("sCast")
    sLN = S("sLN"); sTP = S("sTP"); sFM = S("sFM")
    sQKV = S("sQKV"); sEv = S("sEv")
    sBnG = S("sBnG"); sMV = S("sMV"); sIstd = S("sIstd")
    sBs = S("sBs"); sBt = S("sBt"); sBv = S("sBv"); sBq = S("sBq")
    sS = S("sS"); sMsk = S("sMsk"); sPT = S("sPT"); sPV = S("sPV")
    sLrow = S("sLrow"); sLt = S("sLt"); sRc = S("sRc")
    sOT = S("sOT"); sOP = S("sOP"); sY = S("sY"); sX1 = S("sX1"); sSp = S("sSp")
    sTd = S("sTd"); sVd = S("sVd"); sSd = S("sSd")
    sWB = S("sWB"); sCastB = S("sCastB"); sXnB = S("sXnB"); sX1B = S("sX1B")
    sG = S("sG"); sSil = S("sSil"); sU = S("sU"); sHT = S("sHT")
    sDnT = S("sDnT"); sPw = S("sPw"); sEpB = S("sEpB"); sX1Bt = S("sX1Bt")
    sOut = S("sOut"); sSc = S("sSc"); sVc = S("sVc"); sGq = S("sGq"); sWB2 = S("sWB2")

    pO01 = es.enter_context(nc.psum_tensor("pO01", [128, 512], F32))
    pO23 = es.enter_context(nc.psum_tensor("pO23", [128, 512], F32))
    pAux2 = es.enter_context(nc.psum_tensor("pAux2", [128, 512], F32))
    pAux3 = es.enter_context(nc.psum_tensor("pAux3", [128, 512], F32))
    pS0 = es.enter_context(nc.psum_tensor("pS0", [128, 512], F32))
    pS1 = es.enter_context(nc.psum_tensor("pS1", [128, 512], F32))

    qs = {k: EngQ() for k in ("sync", "tensor", "vector", "scalar", "gpsimd")}

    def W(eng, sem, val):
        if val > 0:
            qs[eng].ops.append(lambda e, s=sem.h, v=val: e.wait_ge(s, v))

    def OP(eng, fn, sem=None, k=1):
        if sem is None:
            qs[eng].ops.append(fn)
            return None
        sem.n += k
        qs[eng].ops.append(lambda e, f=fn, s=sem.h, kk=k: f(e).then_inc(s, kk))
        return sem.n

    esA = ExitStack()
    sb = lambda name, shape, dt: esA.enter_context(nc.sbuf_tensor(name, shape, dt))
    K_s = [sb(f"K{c}", [128, 8192], BF16) for c in range(4)]
    V_s = sb("V", [128, 64 * 512], BF16)
    Wq_s = sb("Wq", [128, 8 * 1536], BF16)
    Wo_s = sb("Wo", [128, 4 * 1024], BF16)
    bout_s = sb("bout", [128, 1024], BF16)
    mask2_s = sb("m2", [128, 256], BF16)
    mask3_s = sb("m3", [128, 256], BF16)
    pc01_s = sb("pc01", [128, 1], F32)
    ident_s = sb("ident", [128, 128], BF16)
    onesc_s = sb("onesc", [128, 1], BF16)
    onesr_s = sb("onesr", [1, 128], F32)
    onef_s = sb("onef", [1, 1], F32)
    econst_s = sb("econst", [128, 2], F32)
    xrot = [sb(f"xr{i}", [128, 1024], F32) for i in range(2)]
    xres_s = sb("xres", [128, 1024], F32)
    xn_s = [sb(f"xn{i}", [128, 1024], BF16) for i in range(2)]
    junk_s = xn_s[0]
    xnfm_s = sb("xnfm", [128, 8 * 256], BF16)
    Qfm_s = sb("Qfm", [128, 4 * 256], BF16)
    PT_s = [sb(f"PT{i}", [128, 256], BF16) for i in range(2)]
    OT_s = sb("OT", [128, 4 * 256], BF16)
    linv_s = sb("linv", [128, 2], F32)
    rows_s = sb("rows", [1, 520], F32)
    bias_s = sb("bias", [128, 4], F32)
    stats_s = sb("stats", [128, 12], F32)
    mv_s = sb("mv", [128, 8], F32)
    istd_s = sb("istd", [128, 8], F32)
    y2_s = sb("y2", [128, 2048], BF16)
    x1o_s = sb("x1o", [128, 1024], F32)
    xnb1_tm = [sb(f"xb1t{i}", [128, 1024], BF16) for i in range(2)]
    xnb1_s = sb("xb1s", [128, 8 * 256], BF16)
    pT = [esA.enter_context(nc.psum_tensor(f"pT{i}", [128, 1024], BF16)) for i in range(2)]

    for dst, src in ((mask2_s, mask2d), (mask3_s, mask3d), (pc01_s, pc01d),
                     (ident_s, identd), (onesc_s, onescd), (onesr_s, onesrd),
                     (onef_s, onefd), (econst_s, econstd)):
        OP("sync", lambda e, d=dst, s=src: e.dma_start(out=d[:, :], in_=s), sW, 16)
    OP("sync", lambda e: e.dma_start(out=x1o_s[:, :], in_=boutb), sW, 16)
    W("scalar", sW, sW.n)
    OP("scalar", lambda e: e.copy(bout_s[:, :], x1o_s[:, :]), sCast, 1)
    stage = [xrot[0], x1o_s]
    si = 0
    for dc in range(8):
        for (c0, w_) in ((0, 1024), (1024, 512)):
            buf = stage[si % 2]
            W("sync", sCast, sCast.n)
            OP("sync", lambda e, b=buf, d=dc, c=c0, w=w_: e.dma_start(
                out=b[:, 0:w], in_=wqkvT[128 * d:128 * d + 128, c:c + w]), sW, 16)
            W("sync", sW, sW.n)
            W("scalar", sW, sW.n)
            OP("scalar", lambda e, b=buf, d=dc, c=c0, w=w_: e.copy(
                Wq_s[:, 1536 * d + c:1536 * d + c + w], b[:, 0:w]), sCast, 1)
            si += 1
    for vc in range(4):
        buf = stage[si % 2]
        W("sync", sCast, sCast.n)
        OP("sync", lambda e, b=buf, v=vc: e.dma_start(
            out=b[:, :], in_=woutT[128 * v:128 * v + 128, :]), sW, 16)
        W("sync", sW, sW.n)
        W("scalar", sW, sW.n)
        OP("scalar", lambda e, b=buf, v=vc: e.copy(
            Wo_s[:, 1024 * v:1024 * v + 1024], b[:, :]), sCast, 1)
        si += 1
    castA_done = sCast.n
    OP("vector", lambda e: e.memset(rows_s[0:1, 514:515], 0.0), sBv, 1)

    ln_cnt = {}
    x_cnt = {}
    expc = {}
    pvc = {}
    spill_after = [0] * (NBLK + 1)
    x1tt = {}
    fm_evac_at = {0: 0, 1: 0}
    ev_at = {0: 0, 1: 0}
    sy_step = [0] * (NBLK + 1)
    st = {"flash_s_prev": 0, "sq_done_prev": 0, "prev_half_qkv": 0}
    tp_xn_at = {0: 0, 1: 0}
    xb1tp_at = {0: 0, 1: 0}
    nb_d = {}
    gp1_d = {}
    gpk_d = {}
    xn0r_d = {}
    xn1r_d = {}
    gpk23_d = {}
    bm_st = {}
    qmx_d = {}
    bn_d = {}
    x1sp = {}
    xnsp = {}
    yfree_d = {}
    istd2_d = {}
    istdb_d = {}
    xr_d = {}
    qkv_ev_d = {}
    bq_d = {}
    flash_done_d = {}
    rc_d = {}

    def stage_LN1(j, subs=(0, 1, 2, 3)):
        for s in subs:
            b = s % 2
            c0 = 2 * b
            prev = ln_cnt.get((j, s - 2), ln_cnt.get((j - 1, s + 2), 0))
            if prev > 0:
                W("sync", sLN, prev)
            if j == 0:
                W("sync", sCast, castA_done)
            r0 = 512 * j + 128 * s
            x_cnt[(j, s)] = OP("sync", lambda e, r=r0, xb=xrot[b]: e.dma_start(
                out=xb[:, :], in_=xc[r:r + 128, :]), sX, 16)
            W("sync", sX, x_cnt[(j, s)])
            W("vector", sX, x_cnt[(j, s)])
            W("vector", sMV, sMV.n)
            OP("vector", lambda e, xb=xrot[b]: e.bn_stats(stats_s[:, 0:6], xb[:, 0:512]), sBnG, 1)
            g = OP("vector", lambda e, xb=xrot[b]: e.bn_stats(stats_s[:, 6:12], xb[:, 512:1024]), sBnG, 1)
            W("vector", sBnG, g)
            mvc = OP("vector", lambda e, c=c0: e.bn_aggr(mv_s[:, c:c + 2], stats_s[:, :]), sMV, 1)
            W("scalar", sMV, mvc)
            lnch = OP("scalar", lambda e, c=c0: e.activation(
                out=istd_s[:, c + 1:c + 2], in_=mv_s[:, c + 1:c + 2],
                func=AF.Ln, bias=econst_s[:, 0:1], scale=1.0), sSc, 1)
            W("scalar", sSc, lnch)
            ic = OP("scalar", lambda e, c=c0: e.activation(
                out=istd_s[:, c:c + 1], in_=istd_s[:, c + 1:c + 2],
                func=AF.Exp, scale=-0.5), sIstd, 1)
            W("vector", sIstd, ic)
            if s == 0 and j > 0:
                W("vector", sGq, gp1_d.get(j - 1, 0))
                W("vector", sBt, xn0r_d.get(j - 1, 0))
            if s == 1 and j > 0:
                W("vector", sGq, gpk_d.get(j - 1, 0))
                W("vector", sBt, xn1r_d.get(j - 1, 0))
            W("vector", sTP, tp_xn_at[b])
            ln_cnt[(j, s)] = OP("vector", lambda e, xb=xrot[b], c=c0, o=xn_s[b]: e.tensor_scalar(
                out=o[:, :], in0=xb[:, :], scalar1=mv_s[:, c:c + 1], scalar2=istd_s[:, c:c + 1],
                op0=ALU.subtract, op1=ALU.mult), sLN, 1)

    def stage_qkv(j):
        for s in range(4):
            b = s % 2
            W("tensor", sLN, ln_cnt[(j, s)])
            W("tensor", sFM, fm_evac_at[b])
            for c in range(8):
                fn = lambda e, cc=c, xb=xn_s[b], pb=pT[b]: e.transpose(
                    pb[:, 128 * cc:128 * cc + 128], xb[:, 128 * cc:128 * cc + 128], ident_s[:, :])
                tpc = OP("tensor", fn, sTP, 1) if c == 7 else OP("tensor", fn)
            tp_xn_at[b] = tpc
            W("vector", sTP, tpc)
            if j == 0 and s == 0:
                W("vector", sCast, castA_done)
            W("vector", sQKV, st["prev_half_qkv"])
            u = s % 2
            src = pT[b].rearrange("p (c w) -> p c w", w=128)
            dst = xnfm_s.rearrange("p (c w) -> p c w", w=256)[:, :, 128 * u:128 * u + 128]
            fm_evac_at[b] = OP("vector", lambda e, d=dst, sr=src: e.tensor_copy(d, sr), sFM, 1)

            if s in (1, 3):
                h = s // 2
                W("tensor", sFM, fm_evac_at[0])
                W("tensor", sFM, fm_evac_at[1])
                if j == 0 and h == 0:
                    W("tensor", sCast, castA_done)
                if h == 0:
                    W("tensor", sY, sy_step[j - 1] if j > 0 else 0)
                groups = [("K", kc) for kc in range(4)] + [("V", v) for v in range(2)]
                if h == 1:
                    groups += [("Q", qc) for qc in range(4)]
                for gi, (kind, idx) in enumerate(groups):
                    bank = (pS0, pS1)[gi % 2]
                    W("tensor", sEv, ev_at[gi % 2])
                    for dc in range(8):
                        if kind == "K":
                            fn = lambda e, d=dc, k=idx, bk=bank: e.matmul(
                                bk[:, 0:256],
                                lhsT=Wq_s[:, 1536 * d + 512 + 128 * k:1536 * d + 512 + 128 * k + 128],
                                rhs=xnfm_s[:, 256 * d:256 * d + 256], start=d == 0, stop=d == 7)
                        elif kind == "V":
                            fn = lambda e, d=dc, v=idx, bk=bank: e.matmul(
                                bk[:, 0:512],
                                lhsT=xnfm_s[:, 256 * d + 128 * v:256 * d + 128 * v + 128],
                                rhs=Wq_s[:, 1536 * d + 1024:1536 * d + 1536], start=d == 0, stop=d == 7)
                        else:
                            fn = lambda e, d=dc, q=idx, bk=bank: e.matmul(
                                bk[:, 0:256],
                                lhsT=Wq_s[:, 1536 * d + 128 * q:1536 * d + 128 * q + 128],
                                rhs=xnfm_s[:, 256 * d:256 * d + 256], start=d == 0, stop=d == 7)
                        qk = OP("tensor", fn, sQKV, 1) if dc == 7 else OP("tensor", fn)
                    W("scalar", sQKV, qk)
                    if kind == "K":
                        col = 512 * j + 256 * h
                        evc = OP("scalar", lambda e, k=idx, cl=col, bk=bank: e.copy(
                            K_s[k][:, cl:cl + 256], bk[:, 0:256]), sEv, 1)
                    elif kind == "V":
                        t = 4 * j + 2 * h + idx
                        evc = OP("scalar", lambda e, t_=t, bk=bank: e.copy(
                            V_s[:, 512 * t_:512 * t_ + 512], bk[:, 0:512]), sEv, 1)
                    else:
                        if idx == 0:
                            W("scalar", sS, st["flash_s_prev"])
                            W("scalar", sGq, gp1_d.get(j - 1, 0))
                        evc = OP("scalar", lambda e, q=idx, bk=bank: e.copy(
                            Qfm_s[:, 256 * q:256 * q + 256], bk[:, 0:256]), sEv, 1)
                    ev_at[gi % 2] = evc
                st["prev_half_qkv"] = sQKV.n
                if h == 0:
                    stage_LN1(j, (2, 3))
                drain_epi(3)
        qkv_ev_d[j] = sEv.n

    def stage_bmach_gp(j):
        W("gpsimd", sTP, tp_xn_at[0])
        W("gpsimd", sLN, ln_cnt[(j, 2)])
        W("gpsimd", sEv, qkv_ev_d[j])
        gp1_d[j] = OP("gpsimd", lambda e: e.tensor_tensor(
            out=junk_s[:, :], in0=Qfm_s[:, :], in1=Qfm_s[:, :], op=ALU.mult), sGq, 1)
        W("gpsimd", sTP, tp_xn_at[1])
        W("gpsimd", sLN, ln_cnt[(j, 3)])
        col = 512 * j
        for i in range(2):
            gpk = OP("gpsimd", lambda e, i_=i, cl=col: e.tensor_tensor(
                out=xn_s[1][:, 512 * i_:512 * i_ + 512], in0=K_s[i_][:, cl:cl + 512],
                in1=K_s[i_][:, cl:cl + 512], op=ALU.mult), sGq, 1)
        gpk_d[j] = gpk
        W("gpsimd", sOP, op_d.get((j - 1, 1, 1), 0))
        W("gpsimd", sOT, 2 * j)
        for i in range(2):
            gpk23 = OP("gpsimd", lambda e, i_=i, cl=col: e.tensor_tensor(
                out=OT_s[:, 512 * i_:512 * i_ + 512], in0=K_s[2 + i_][:, cl:cl + 512],
                in1=K_s[2 + i_][:, cl:cl + 512], op=ALU.mult), sGq, 1)
        gpk23_d[j] = gpk23

    def stage_bmach_mm1(j):
        W("tensor", sGq, gp1_d[j])
        if j > 0:
            W("tensor", sLrow, j + 1)
        for c in range(4):
            qmm = OP("tensor", lambda e, cc=c: e.matmul(
                pAux2[0:1, 0:256], lhsT=onesc_s[:, :],
                rhs=junk_s[:, 256 * cc:256 * cc + 256],
                start=cc == 0, stop=cc == 3), sBt, 1)
        xn0r_d[j] = qmm
        W("tensor", sGq, gpk_d[j])
        W("tensor", sBs, bn_d.get(j - 1, 0))
        for i in range(2):
            OP("tensor", lambda e, i_=i: e.matmul(
                pAux3[0:1, 0:512], lhsT=onesc_s[:, :],
                rhs=xn_s[1][:, 512 * i_:512 * i_ + 512],
                start=i_ == 0, stop=False), sBt, 1)
        W("tensor", sGq, gpk23_d[j])
        for i in range(2):
            kmm = OP("tensor", lambda e, i_=i: e.matmul(
                pAux3[0:1, 0:512], lhsT=onesc_s[:, :],
                rhs=OT_s[:, 512 * i_:512 * i_ + 512],
                start=False, stop=i_ == 1), sBt, 1)
        xn1r_d[j] = kmm
        W("vector", sBt, qmm)
        qmx = OP("vector", lambda e: e.reduce_max(rows_s[0:1, 512:513], pAux2[0:1, 0:256],
                                                  axis=AX.X), sBv, 1)
        qmx_d[j] = qmx
        W("vector", sBt, kmm)
        kb = OP("vector", lambda e: e.reduce_max(rows_s[0:1, 513:514], pAux3[0:1, 0:512],
                                                 axis=AX.X), sBv, 1)
        W("vector", sBv, kb)
        kmx = OP("vector", lambda e: e.tensor_tensor(out=rows_s[0:1, 514:515], in0=rows_s[0:1, 513:514],
                                                     in1=rows_s[0:1, 514:515], op=ALU.max), sBv, 1)
        W("vector", sBv, kmx)
        bv = OP("vector", lambda e: e.tensor_tensor(out=rows_s[0:1, 515:516], in0=rows_s[0:1, 512:513],
                                                    in1=rows_s[0:1, 514:515], op=ALU.mult), sBv, 1)
        bm_st[j] = bv

    def stage_bmach_mm2(j):
        bv = bm_st[j]
        W("scalar", sBv, bv)
        lnch = OP("scalar", lambda e: e.activation(out=rows_s[0:1, 516:517], in_=rows_s[0:1, 515:516],
                                                   func=AF.Ln, scale=1.0), sSc, 1)
        W("scalar", sSc, lnch)
        nb_d[j] = OP("scalar", lambda e: e.activation(out=rows_s[0:1, 517:518], in_=rows_s[0:1, 516:517],
                                                      func=AF.Exp, bias=econst_s[0:1, 1:2], scale=0.5), sBs, 1)

    def stage_bmachB2(j):
        W("tensor", sBs, nb_d[j])
        bc = OP("tensor", lambda e: e.matmul(pAux3[:, 0:1], lhsT=onesr_s[0:1, :],
                                             rhs=rows_s[0:1, 517:518], start=True, stop=True), sBt, 1)
        W("scalar", sBt, bc)
        bsl = 2 * (j % 2)
        bn = OP("scalar", lambda e, b0=bsl: e.copy(bias_s[:, b0:b0 + 1], pAux3[:, 0:1]), sBs, 1)
        bn_d[j] = bn
        W("vector", sBs, bn)
        bq_d[j] = OP("vector", lambda e, b0=bsl: e.tensor_tensor(out=bias_s[:, b0 + 1:b0 + 2],
                                                                 in0=bias_s[:, b0:b0 + 1],
                                                                 in1=pc01_s[:, :], op=ALU.add), sBq, 1)

    def stage_flash(j):
        ntile = 4 * j + 4
        W("scalar", sBq, max(j, 1))

        def emit_S(t):
            bank = (pS0, pS1)[t % 2]
            if t >= 2:
                W("tensor", sPT, expc[(j, t - 2)])
            else:
                W("tensor", sEv, qkv_ev_d[j])
            for c in range(4):
                fn = lambda e, cc=c, t_=t, bk=bank: e.matmul(
                    bk[:, 0:256], lhsT=K_s[cc][:, 128 * t_:128 * t_ + 128],
                    rhs=Qfm_s[:, 256 * cc:256 * cc + 256], start=cc == 0, stop=cc == 3)
                sg = OP("tensor", fn, sS, 1) if c == 3 else OP("tensor", fn)
            diag = t >= 4 * j + 2
            if diag:
                W("vector", sS, sg)
                msk = mask2_s if t == 4 * j + 2 else mask3_s
                mk = OP("vector", lambda e, bk=bank, m=msk: e.tensor_tensor(
                    out=bk[:, 0:256], in0=bk[:, 0:256], in1=m[:, :], op=ALU.add), sMsk, 1)
                W("scalar", sMsk, mk)
            else:
                W("scalar", sS, sg)
            if t >= 2:
                W("scalar", sPV, pvc[(j, t - 2)])
            bp = 2 * (max(j - 1, 0) % 2)
            bias_ap = bias_s[:, bp + 1:bp + 2] if (4 * j <= t < 4 * j + 2) else bias_s[:, bp:bp + 1]
            expc[(j, t)] = OP("scalar", lambda e, bk=bank, ba=bias_ap, pt=PT_s[t % 2]: e.activation(
                out=pt[:, :], in_=bk[:, 0:256], func=AF.Exp, bias=ba, scale=SCALE), sPT, 1)

        def emit_PV(t):
            W("tensor", sPT, expc[(j, t)])
            if t == 0:
                W("tensor", sY, sy_step[j - 1] if j > 0 else 0)
                W("tensor", sLrow, j)
                W("tensor", sBv, qmx_d.get(j - 1, qmx_d.get(0, 0)))
                W("tensor", sRc, rc_d.get(j - 1, 0))
            first = t == 0
            lastt = t == ntile - 1
            for c in range(4):
                pb = pO01 if c < 2 else pO23
                cc0 = 256 * (c % 2)
                stf = first and (c % 2 == 0)
                OP("tensor", lambda e, t_=t, cc=c, pb_=pb, c0=cc0, st_=stf, sp_=lastt: e.matmul(
                    pb_[:, c0:c0 + 256], lhsT=V_s[:, 512 * t_ + 128 * cc:512 * t_ + 128 * cc + 128],
                    rhs=PT_s[t_ % 2][:, :], start=st_, stop=sp_, skip_group_check=True))
            pvc[(j, t)] = OP("tensor", lambda e, t_=t, st_=first, sp_=lastt: e.matmul(
                pAux2[0:1, 0:256], lhsT=onesc_s[:, :], rhs=PT_s[t_ % 2][:, :],
                start=st_, stop=sp_), sPV, 1)

        for t in range(ntile):
            emit_S(t)
            if t >= 1:
                emit_PV(t - 1)
        emit_PV(ntile - 1)
        st["flash_s_prev"] = sS.n
        flash_done_d[j] = pvc[(j, ntile - 1)]

    def stage_out(j):
        ntile = 4 * j + 4
        flash_done = flash_done_d[j]
        W("scalar", sPV, flash_done)
        W("scalar", sLt, 2 * j)
        lr = OP("scalar", lambda e: e.copy(rows_s[0:1, 0:256], pAux2[0:1, 0:256]), sLrow, 1)
        W("tensor", sLrow, lr)
        W("tensor", sBv, qmx_d.get(j, 0))
        W("tensor", sRc, rc_d.get(j - 1, 0))
        for u in range(2):
            lt = OP("tensor", lambda e, u_=u: e.matmul(
                pAux2[:, 256 + u_:257 + u_], lhsT=rows_s[0:1, 128 * u_:128 * u_ + 128],
                rhs=onef_s[0:1, 0:1], start=True, stop=True), sLt, 1)
        W("vector", sLt, lt)
        rc_d[j] = OP("vector", lambda e: e.reciprocal(linv_s[:, 0:2], pAux2[:, 256:258]), sRc, 1)
        W("scalar", sBt, xn1r_d.get(j, 0))
        ot = OP("scalar", lambda e: e.copy(OT_s[:, 0:512], pO01[:, :]), sOT, 1)
        ot = OP("scalar", lambda e: e.copy(OT_s[:, 512:1024], pO23[:, :]), sOT, 1)
        for u in range(2):
            for hh in range(2):
                pb = (pO01, pO23)[hh] if u == 0 else (pS0, pS1)[hh]
                W("tensor", sOT, ot)
                if u == 1 and hh == 0:
                    W("tensor", sPT, expc[(j, ntile - 1)])
                for c in range(4):
                    fn = lambda e, cc=c, u_=u, h_=hh, pb_=pb: e.matmul(
                        pb_[:, 0:512], lhsT=OT_s[:, 256 * cc + 128 * u_:256 * cc + 128 * u_ + 128],
                        rhs=Wo_s[:, 1024 * cc + 512 * h_:1024 * cc + 512 * h_ + 512],
                        start=cc == 0, stop=cc == 3)
                    opc = OP("tensor", fn, sOP, 1) if c == 3 else OP("tensor", fn)
                op_d[(j, u, hh)] = opc

    def stage_epi_stt(j):
        for u in range(2):
            for hh in range(2):
                pb = (pO01, pO23)[hh] if u == 0 else (pS0, pS1)[hh]
                W("vector", sOP, op_d[(j, u, hh)])
                if u == 0 and hh == 0:
                    W("vector", sRc, rc_d[j])
                    if j == 0:
                        W("vector", sCast, castA_done)
                    W("vector", sLN, yfree_d.get(j - 1, 0))
                OP("vector", lambda e, u_=u, h_=hh, pb_=pb: e.scalar_tensor_tensor(
                    out=y2_s[:, 1024 * u_ + 512 * h_:1024 * u_ + 512 * h_ + 512], in0=pb_[:, 0:512],
                    scalar=linv_s[:, u_:u_ + 1], in1=bout_s[:, 512 * h_:512 * h_ + 512],
                    op0=ALU.mult, op1=ALU.add), sY, 1)
        sy_step[j] = sY.n

    epi_pend = []

    def drain_epi(n):
        k = 0
        while epi_pend and k < n:
            epi_pend.pop(0)()
            k += 1

    def stage_epi_prepare(j):
        def ln2_piece(u):
            def f():
                W("vector", sY, sy_step[j])
                W("vector", sMV, sMV.n)
                OP("vector", lambda e, u_=u: e.bn_stats(stats_s[:, 0:6], y2_s[:, 1024 * u_:1024 * u_ + 512]), sBnG, 1)
                g = OP("vector", lambda e, u_=u: e.bn_stats(stats_s[:, 6:12], y2_s[:, 1024 * u_ + 512:1024 * u_ + 1024]), sBnG, 1)
                W("vector", sBnG, g)
                mvc = OP("vector", lambda e: e.bn_aggr(mv_s[:, 4:6], stats_s[:, :]), sMV, 1)
                W("scalar", sMV, mvc)
                lnch = OP("scalar", lambda e: e.activation(out=istd_s[:, 5:6], in_=mv_s[:, 5:6],
                                                    func=AF.Ln, bias=econst_s[:, 0:1], scale=1.0), sSc, 1)
                W("scalar", sSc, lnch)
                ic = OP("scalar", lambda e: e.activation(out=istd_s[:, 4:5], in_=istd_s[:, 5:6],
                                                         func=AF.Exp, scale=-0.5), sIstd, 1)
                istd2_d[(j, u)] = ic
            return f

        def x1_piece(u):
            def f():
                W("vector", sIstd, istd2_d[(j, u)])
                prevsp = x1sp.get((j, u - 1), x1sp.get((j - 1, 1), 0))
                W("vector", sSp, prevsp)
                vch = OP("vector", lambda e, u_=u: e.tensor_scalar(
                    out=x1o_s[:, :], in0=y2_s[:, 1024 * u_:1024 * u_ + 1024], scalar1=mv_s[:, 4:5],
                    scalar2=istd_s[:, 4:5], op0=ALU.subtract, op1=ALU.mult), sVc, 1)
                W("vector", sVc, vch)
                W("vector", sXr, xr_d[(j, u)])
                x1tt[(j, u)] = OP("vector", lambda e: e.tensor_tensor(
                    out=x1o_s[:, :], in0=x1o_s[:, :], in1=xres_s[:, :], op=ALU.add), sX1, 1)
                W("vector", sX1, x1tt[(j, u)])
                W("vector", sMV, sMV.n)
                OP("vector", lambda e: e.bn_stats(stats_s[:, 0:6], x1o_s[:, 0:512]), sBnG, 1)
                g = OP("vector", lambda e: e.bn_stats(stats_s[:, 6:12], x1o_s[:, 512:1024]), sBnG, 1)
                W("vector", sBnG, g)
                mvc = OP("vector", lambda e: e.bn_aggr(mv_s[:, 6:8], stats_s[:, :]), sMV, 1)
                W("scalar", sMV, mvc)
                lnch = OP("scalar", lambda e: e.activation(out=istd_s[:, 7:8], in_=mv_s[:, 7:8],
                                                    func=AF.Ln, bias=econst_s[:, 0:1], scale=1.0), sSc, 1)
                W("scalar", sSc, lnch)
                ic = OP("scalar", lambda e: e.activation(out=istd_s[:, 6:7], in_=istd_s[:, 7:8],
                                                         func=AF.Exp, scale=-0.5), sIstd, 1)
                istdb_d[(j, u)] = ic
            return f

        def norm_piece(u):
            def f():
                W("vector", sIstd, istdb_d[(j, u)])
                W("vector", sTP, xb1tp_at[u])
                ln_cnt[(j, 4 + u)] = OP("vector", lambda e, u_=u: e.tensor_scalar(
                    out=xnb1_tm[u_][:, :], in0=x1o_s[:, :], scalar1=mv_s[:, 6:7], scalar2=istd_s[:, 6:7],
                    op0=ALU.subtract, op1=ALU.mult), sLN, 1)
                if u == 1:
                    yfree_d[j] = ln_cnt[(j, 5)]
                W("gpsimd", sX1, x1tt[(j, u)])
                r0 = 256 * j + 128 * u
                x1sp[(j, u)] = OP("gpsimd", lambda e, r=r0: e.dma_start(out=x1s[r:r + 128, :], in_=x1o_s[:, :]), sSp, 16)
                W("gpsimd", sSp, sSp.n)
                if u == 0:
                    r1 = 512 * j + 256 + 128
                elif j + 1 < NBLK:
                    r1 = 512 * (j + 1) + 256
                else:
                    r1 = None
                if r1 is not None:
                    OP("gpsimd", lambda e, r=r1: e.dma_start(out=xres_s[:, :], in_=xc[r:r + 128, :]), sXr, 16)
                    W("gpsimd", sXr, sXr.n)
                    xr_d[(j + u, 1 - u)] = sXr.n
            return f

        for u in range(2):
            epi_pend.append(ln2_piece(u))
            epi_pend.append(x1_piece(u))
            epi_pend.append(norm_piece(u))

    def stage_xnb1(j):
        for u in range(2):
            b = u % 2
            W("tensor", sLN, ln_cnt[(j, 4 + u)])
            W("tensor", sFM, fm_evac_at[b])
            for c in range(8):
                fn = lambda e, cc=c, u_=u, pb=pT[b]: e.transpose(
                    pb[:, 128 * cc:128 * cc + 128], xnb1_tm[u_][:, 128 * cc:128 * cc + 128], ident_s[:, :])
                tpb = OP("tensor", fn, sTP, 1) if c == 7 else OP("tensor", fn)
            xb1tp_at[u] = tpb
            W("vector", sTP, tpb)
            W("vector", sSp, xnsp.get(j - 1, 0))
            src = pT[b].rearrange("p (c w) -> p c w", w=128)
            dst = xnb1_s.rearrange("p (c w) -> p c w", w=256)[:, :, 128 * u:128 * u + 128]
            fm_evac_at[b] = OP("vector", lambda e, d=dst, sr=src: e.tensor_copy(d, sr), sFM, 1)
        W("gpsimd", sFM, fm_evac_at[1])
        for c in range(8):
            OP("gpsimd", lambda e, cc=c, jj=j: e.dma_start(
                out=xnfms[cc, :, 256 * jj:256 * jj + 256], in_=xnb1_s[:, 256 * cc:256 * cc + 256]), sSp, 16)
        xnsp[j] = sSp.n
        W("gpsimd", sSp, xnsp[j])

    op_d = {}
    OP("gpsimd", lambda e: e.dma_start(out=xres_s[:, :], in_=xc[256:384, :]), sXr, 16)
    W("gpsimd", sXr, sXr.n)
    xr_d[(0, 0)] = sXr.n
    stage_LN1(0, (0, 1))
    stage_qkv(0)
    drain_epi(99)
    stage_bmach_gp(0)
    stage_bmach_mm1(0)
    stage_bmach_mm2(0)
    stage_bmachB2(0)
    for j in range(NBLK):
        if j >= 2:
            stage_bmachB2(j - 1)
        if 1 <= j < NBLK - 1:
            stage_bmach_gp(j)
        stage_flash(j)
        if 1 <= j < NBLK - 1:
            stage_bmach_mm1(j)
        stage_out(j)
        if 1 <= j < NBLK - 1:
            stage_bmach_mm2(j)
        if j + 1 < NBLK:
            stage_LN1(j + 1, (0, 1))
        stage_epi_stt(j)
        stage_epi_prepare(j)
        if j >= 1:
            stage_xnb1(j - 1)
        if j + 1 < NBLK:
            stage_qkv(j + 1)
        drain_epi(99)
    stage_xnb1(NBLK - 1)

    tD = OP("tensor", lambda e: e.nop(), sTd, 1)
    vD = OP("vector", lambda e: e.nop(), sVd, 1)
    scD = OP("scalar", lambda e: e.nop(), sSd, 1)
    esA.close()

    esB = ExitStack()
    sbB = lambda name, shape, dt: esB.enter_context(nc.sbuf_tensor(name, shape, dt))
    Wg_s = sbB("Wg", [128, HC * 1024], BF16)
    Wu_s = sbB("Wu", [128, HC * 1024], BF16)
    Wd_s = sbB("Wd", [128, HC * 1024], BF16)
    wstgB = [sbB(f"wstg{i}", [128, 1024], I32) for i in range(4)]
    xnfmB = [sbB(f"xnB{i}", [128, 8 * 256], BF16) for i in range(2)]
    x1B = [sbB(f"x1B{i}", [128, 2048], F32) for i in range(2)]
    gsilu = [sbB(f"gs{i}", [128, 256], BF16) for i in range(2)]
    hT_s = [sbB(f"hT{i}", [128, 256], BF16) for i in range(2)]
    outB = [sbB(f"oB{i}", [128, 1024], F32) for i in range(2)]
    y3B = [sbB(f"y3B{i}", [128, 1024], F32) for i in range(2)]
    statsB = sbB("stB", [128, 12], F32)
    mvB = sbB("mvB", [128, 4], F32)
    vpe_s = sbB("vpe", [128, 2], F32)
    istdB = sbB("isB", [128, 2], F32)
    nhalf = sbB("nh", [128, 1], F32)
    pB6 = esB.enter_context(nc.psum_tensor("pB6", [128, 512], F32))
    pB7 = esB.enter_context(nc.psum_tensor("pB7", [128, 512], F32))
    pDn = [pO01, pO23, pAux2, pAux3]
    pG = [pS0, pB6]
    pU = [pS1, pB7]

    _all_sems = [sX, sXr, sW, sCast, sLN, sTP, sFM, sQKV, sEv, sBnG, sMV, sIstd,
                 sBs, sBt, sBv, sBq, sS, sMsk, sPT, sPV, sLrow, sLt, sRc, sOT,
                 sOP, sY, sX1, sSp, sSc, sVc, sGq, sTd, sVd, sSd]
    for e_ in ("sync", "gpsimd", "scalar", "vector", "tensor"):
        for _sm in _all_sems:
            W(e_, _sm, _sm.n)
    W("sync", sSp, xnsp[NBLK - 1])
    OP("gpsimd", lambda e: e.memset(nhalf[:, :], -0.5), sPw, 1)

    castB = {}
    wi = [0]

    wiC = [0]

    def issue_w(w, hc):
        k = wi[0]
        ring = "sync" if k % 2 == 0 else "gpsimd"
        sem = sWB if k % 2 == 0 else sWB2
        buf = wstgB[k % 4]
        W(ring, sCastB, max(0, k - 3))
        W(ring, sem, 16 * (k // 2))
        src = (wgT, wuT, wdT)[w]
        OP(ring, lambda e, b=buf, s=src, h=hc: e.dma_start(out=b[:, :], in_=s[h, :, :]), sem, 16)
        wi[0] += 1

    def cast_w(w, hc):
        k = wiC[0]
        buf = wstgB[k % 4]
        W("scalar", sWB if k % 2 == 0 else sWB2, 16 * (k // 2 + 1))
        dstw = (Wg_s, Wu_s, Wd_s)[w]
        castB[(w, hc)] = OP("scalar", lambda e, b=buf, d=dstw, h=hc: e.activation(
            out=d[:, 1024 * h:1024 * h + 1024], in_=b[:, :], func=AF.Copy, bias=0.0, scale=QDEQ),
            sCastB, 1)
        wiC[0] += 1

    xn_ld = [0] * NBLK
    x1_ld = [0] * NBLK
    y3_read = {}
    ln3_pend = {}

    def emit_ln3_tail(g):
        pw0, pw1 = ln3_pend.pop(g)
        for u in range(2):
            W("vector", sPw, (pw0, pw1)[u])
            W("vector", sOut, st_cnt.get((g - 1, u), 0))
            for hh in range(2):
                y3v = OP("vector", lambda e, u_=u, h_=hh: e.tensor_scalar(
                    out=outB[u_][:, 512 * h_:512 * h_ + 512], in0=y3B[u_][:, 512 * h_:512 * h_ + 512],
                    scalar1=mvB[:, 2 * u_:2 * u_ + 1], scalar2=istdB[:, u_:u_ + 1],
                    op0=ALU.subtract, op1=ALU.mult), sVc, 1)
            y3_read[g] = y3v
            W("vector", sVc, y3v)
            W("vector", sX1B, x1_ld[g])
            tt = OP("vector", lambda e, u_=u, gg=g: e.tensor_tensor(
                out=outB[u_][:, :], in0=outB[u_][:, :], in1=x1B[gg % 2][:, 1024 * u_:1024 * u_ + 1024],
                op=ALU.add), sX1Bt, 1)
            W("gpsimd", sX1Bt, tt)
            r0 = 256 * g + 128 * u
            st_cnt[(g, u)] = OP("gpsimd", lambda e, r=r0, u_=u: e.dma_start(
                out=out_c[r:r + 128, :], in_=outB[u_][:, :]), sOut, 16)
            W("gpsimd", sOut, st_cnt[(g, u)])
    up_cnt = {}
    sil_at = {0: 0, 1: 0}
    ht_at = {0: 0, 1: 0}
    ep_cnt = [0] * (NBLK + 1)
    st_cnt = {}

    def emit_loads(g):
        if g >= 2:
            W("sync", sU, up_cnt[(g - 2, HC - 1)])
        for c in range(8):
            OP("sync", lambda e, cc=c, gg=g: e.dma_start(
                out=xnfmB[gg % 2][:, 256 * cc:256 * cc + 256],
                in_=xnfms[cc, :, 256 * gg:256 * gg + 256]), sXnB, 16)
        xn_ld[g] = sXnB.n
        W("sync", sXnB, xn_ld[g])
        W("sync", sSp, xnsp[g])
        if g >= 2:
            W("sync", sX1Bt, 2 * (g - 1))
        for u in range(2):
            r0 = 256 * g + 128 * u
            OP("sync", lambda e, r=r0, u_=u, gg=g: e.dma_start(
                out=x1B[gg % 2][:, 1024 * u_:1024 * u_ + 1024], in_=x1s[r:r + 128, :]), sX1B, 16)
        x1_ld[g] = sX1B.n
        W("sync", sX1B, x1_ld[g])

    for h_ in range(2):
        for w_ in range(3):
            issue_w(w_, h_)
    emit_loads(0)
    emit_loads(1)
    for g in range(NBLK):

        def emit_down(hc):
            bk2 = hc % 2
            W("tensor", sHT, ht_at_rec[hc])
            W("tensor", sCastB, castB[(2, hc)])
            if hc == 0:
                W("tensor", sEpB, ep_cnt[g - 1] if g > 0 else 0)
            dn_ = None
            for u in range(2):
                for hh in range(2):
                    fn = lambda e, h=hc, u_=u, h_=hh, b=bk2: e.matmul(
                        pDn[2 * u_ + h_][:, 0:512], lhsT=hT_s[b][:, 128 * u_:128 * u_ + 128],
                        rhs=Wd_s[:, 1024 * h + 512 * h_:1024 * h + 512 * h_ + 512],
                        start=h == 0, stop=h == HC - 1)
                    dn_ = OP("tensor", fn, sDnT, 1) if (u == 1 and hh == 1) else OP("tensor", fn)
            return dn_

        ht_at_rec = {}
        dn_at_rec = {}
        for hc in range(HC):
            bk = hc % 2
            if g == 0:
                if hc + 2 < HC:
                    for w_ in range(3):
                        issue_w(w_, hc + 2)
                for w_ in range(3):
                    cast_w(w_, hc)
            W("tensor", sCastB, castB[(0, hc)])
            W("tensor", sXnB, xn_ld[g])
            W("tensor", sSil, sil_at[bk])
            for dc in range(8):
                fn = lambda e, d=dc, h=hc, b=bk, gg=g: e.matmul(
                    pG[b][:, 0:256], lhsT=Wg_s[:, 1024 * h + 128 * d:1024 * h + 128 * d + 128],
                    rhs=xnfmB[gg % 2][:, 256 * d:256 * d + 256], start=d == 0, stop=d == 7)
                gq = OP("tensor", fn, sG, 1) if dc == 7 else OP("tensor", fn)
            W("scalar", sG, gq)
            W("scalar", sHT, ht_at[bk])
            sil_at[bk] = OP("scalar", lambda e, b=bk: e.activation(
                out=gsilu[b][:, :], in_=pG[b][:, 0:256], func=AF.Silu), sSil, 1)
            W("tensor", sCastB, castB[(1, hc)])
            for dc in range(8):
                fn = lambda e, d=dc, h=hc, b=bk, gg=g: e.matmul(
                    pU[b][:, 0:256], lhsT=Wu_s[:, 1024 * h + 128 * d:1024 * h + 128 * d + 128],
                    rhs=xnfmB[gg % 2][:, 256 * d:256 * d + 256], start=d == 0, stop=d == 7)
                uq = OP("tensor", fn, sU, 1) if dc == 7 else OP("tensor", fn)
            up_cnt[(g, hc)] = uq
            if hc >= 1:
                dn = emit_down(hc - 1)
                dn_at_rec[hc - 1] = dn
            W("vector", sU, uq)
            W("vector", sSil, sil_at[bk])
            if hc >= 2:
                W("vector", sDnT, dn_at_rec[hc - 2])
            ht_at[bk] = OP("vector", lambda e, b=bk: e.tensor_tensor(
                out=hT_s[b][:, :], in0=pU[b][:, 0:256], in1=gsilu[b][:, :], op=ALU.mult), sHT, 1)
            ht_at_rec[hc] = ht_at[bk]
            if hc == 4 and (g - 1) in ln3_pend:
                emit_ln3_tail(g - 1)
        dn = emit_down(HC - 1)

        W("vector", sDnT, dn)
        W("vector", sVc, y3_read[g - 1] if g > 0 else 0)
        ev3u = [0, 0]
        for u in range(2):
            for hh in range(2):
                ev3u[u] = OP("vector", lambda e, u_=u, h_=hh: e.tensor_copy(
                    y3B[u_][:, 512 * h_:512 * h_ + 512], pDn[2 * u_ + h_][:, 0:512]), sEpB, 1)
        pw_d = [0, 0]
        for u in range(2):
            W("vector", sEpB, ev3u[u])
            W("vector", sMV, sMV.n)
            OP("vector", lambda e, u_=u: e.bn_stats(statsB[:, 0:6], y3B[u_][:, 0:512]), sBnG, 1)
            gg2 = OP("vector", lambda e, u_=u: e.bn_stats(statsB[:, 6:12], y3B[u_][:, 512:1024]), sBnG, 1)
            W("vector", sBnG, gg2)
            mvc = OP("vector", lambda e, u_=u: e.bn_aggr(mvB[:, 2 * u_:2 * u_ + 2], statsB[:, :]), sMV, 1)
            W("vector", sMV, mvc)
            vp = OP("vector", lambda e, u_=u: e.tensor_scalar(
                out=vpe_s[:, u_:u_ + 1], in0=mvB[:, 2 * u_ + 1:2 * u_ + 2], scalar1=EPS, scalar2=None,
                op0=ALU.add), sMV, 1)
            W("gpsimd", sMV, vp)
            W("gpsimd", sPw, 1)
            pw_d[u] = OP("gpsimd", lambda e, u_=u: e.tensor_tensor(
                out=istdB[:, u_:u_ + 1], in0=vpe_s[:, u_:u_ + 1], in1=nhalf[:, :], op=ALU.pow), sPw, 1)
        ln3_pend[g] = (pw_d[0], pw_d[1])
        ep_cnt[g] = sEpB.n
        if g + 2 < NBLK:
            emit_loads(g + 2)

    if (NBLK - 1) in ln3_pend:
        emit_ln3_tail(NBLK - 1)
    W("gpsimd", sOut, sOut.n)

    block = es.enter_context(nc.Block())
    block.sync(qs["sync"].run)
    block.tensor(qs["tensor"].run)
    block.vector(qs["vector"].run)
    block.scalar(qs["scalar"].run)
    block.gpsimd(qs["gpsimd"].run)
    esB.close()
    es.close()
    return nc


def _host_inputs(x, wqkv, wout, bout, wg, wu, wd):
    x = np.ascontiguousarray(np.asarray(x), dtype=np.float32)
    wqkvT = np.ascontiguousarray(np.asarray(wqkv, dtype=np.float32).T)
    woutT = np.ascontiguousarray(np.asarray(wout, dtype=np.float32).T)
    boutb = np.ascontiguousarray(np.broadcast_to(np.asarray(bout, dtype=np.float32)[None, :], (128, D)))
    wg = np.asarray(wg, dtype=np.int32)
    wu = np.asarray(wu, dtype=np.int32)
    wd = np.asarray(wd, dtype=np.int32)
    wgT = np.ascontiguousarray(wg.reshape(HC, 128, 8, 128).transpose(0, 3, 2, 1)).reshape(HC, 128, 1024)
    wuT = np.ascontiguousarray(wu.reshape(HC, 128, 8, 128).transpose(0, 3, 2, 1)).reshape(HC, 128, 1024)
    wdT = np.ascontiguousarray(wd.T).reshape(HC, 128, 1024)
    kk = np.arange(128)[:, None]
    qq = np.arange(256)[None, :]
    mask2 = np.where(kk <= qq, 0.0, NEG).astype(BF)
    mask3 = np.where(kk + 128 <= qq, 0.0, NEG).astype(BF)
    ident = np.eye(128, dtype=BF)
    onesc = np.ones((128, 1), dtype=BF)
    onesr = np.full((1, 128), -1.0, dtype=np.float32)
    onef = np.ones((1, 1), dtype=np.float32)
    econst = np.ascontiguousarray(
        np.broadcast_to(np.array([EPS, LNSCALE], dtype=np.float32)[None, :], (128, 2)))
    shared = dict(wqkvT=wqkvT, woutT=woutT, boutb=boutb, wgT=wgT, wuT=wuT, wdT=wdT,
                  mask2d=mask2, mask3d=mask3, identd=ident, onescd=onesc,
                  onesrd=onesr, onefd=onef, econstd=econst)
    in_maps = []
    for c in range(8):
        b, p = c // 2, c % 2
        if p == 1:
            xb = x[b]
        else:
            xb = x[b].reshape(NBLK, 2, 256, D)[:, ::-1].reshape(8192, D)
        m = dict(shared)
        m["xc"] = np.ascontiguousarray(xb)
        m["pc01d"] = np.full((128, 1), NEG if p == 0 else 0.0, dtype=np.float32)
        in_maps.append(m)
    return in_maps


def kernel(L0_attn_x, L0_attn_w_qkv, L0_attn_w_out, L0_attn_b_out,
           L0_ffn_w_gate, L0_ffn_w_up, L0_ffn_w_down):
    in_maps = _host_inputs(L0_attn_x, L0_attn_w_qkv, L0_attn_w_out, L0_attn_b_out,
                           L0_ffn_w_gate, L0_ffn_w_up, L0_ffn_w_down)
    nc = build_nc()
    res = run_bass_kernel_spmd(nc, in_maps, core_ids=list(range(8)))
    out = np.empty((4, 8192, D), dtype=np.float32)
    for c in range(8):
        b, p = c // 2, c % 2
        r = np.asarray(res.results[c]["out_c"])
        out[b].reshape(NBLK, 2, 256, D)[:, p] = r.reshape(NBLK, 256, D)
    return out


if __name__ == "__main__":
    import reference as _r
    ins = _r.setup_inputs()
    out = kernel(**{k: np.asarray(v) for k, v in ins.items()})
    ref = np.asarray(_r.reference(**ins))
    err = float(np.sqrt(((out - ref) ** 2).mean() / (ref ** 2).mean()))
    print("rel l2:", err)


# version 28
# speedup vs baseline: 1.0411x; 1.0411x over previous
import numpy as np
import ml_dtypes
import concourse.bass as bass
import concourse.mybir as mybir
from concourse.bass_utils import run_bass_kernel_spmd
from contextlib import ExitStack

F32 = mybir.dt.float32
BF16 = mybir.dt.bfloat16
I32 = mybir.dt.int32
AF = mybir.ActivationFunctionType
ALU = mybir.AluOpType
AX = mybir.AxisListType

NBLK = 16
D = 1024
HC = 24
SCALE = 512.0 ** -0.5
LNSCALE = float(np.log(SCALE * 1.25))
EPS = 1e-5
NEG = -1e30
QDEQ = 1.0 / 127.0
BF = ml_dtypes.bfloat16


class Sem:
    def __init__(self, es, nc, name):
        self.h = es.enter_context(nc.semaphore(name))
        self.n = 0


class EngQ:
    def __init__(self):
        self.ops = []

    def run(self, eng):
        for op in self.ops:
            op(eng)


def build_nc():
    nc = bass.Bass("TRN2", target_bir_lowering=False, debug=False)

    xc = nc.dram_tensor("xc", [8192, D], F32, kind="ExternalInput").ap()
    wqkvT = nc.dram_tensor("wqkvT", [D, 1536], F32, kind="ExternalInput").ap()
    woutT = nc.dram_tensor("woutT", [512, D], F32, kind="ExternalInput").ap()
    boutb = nc.dram_tensor("boutb", [128, D], F32, kind="ExternalInput").ap()
    wgT = nc.dram_tensor("wgT", [HC, 128, 1024], I32, kind="ExternalInput").ap()
    wuT = nc.dram_tensor("wuT", [HC, 128, 1024], I32, kind="ExternalInput").ap()
    wdT = nc.dram_tensor("wdT", [HC, 128, 1024], I32, kind="ExternalInput").ap()
    mask2d = nc.dram_tensor("mask2d", [128, 256], BF16, kind="ExternalInput").ap()
    mask3d = nc.dram_tensor("mask3d", [128, 256], BF16, kind="ExternalInput").ap()
    pc01d = nc.dram_tensor("pc01d", [128, 1], F32, kind="ExternalInput").ap()
    identd = nc.dram_tensor("identd", [128, 128], BF16, kind="ExternalInput").ap()
    onescd = nc.dram_tensor("onescd", [128, 1], BF16, kind="ExternalInput").ap()
    onesrd = nc.dram_tensor("onesrd", [1, 128], F32, kind="ExternalInput").ap()
    onefd = nc.dram_tensor("onefd", [1, 1], F32, kind="ExternalInput").ap()
    econstd = nc.dram_tensor("econstd", [128, 2], F32, kind="ExternalInput").ap()
    out_c = nc.dram_tensor("out_c", [4096, D], F32, kind="ExternalOutput").ap()
    x1s = nc.dram_tensor("x1s", [4096, D], F32, kind="Internal").ap()
    xnfms = nc.dram_tensor("xnfms", [8, 128, 4096], BF16, kind="Internal").ap()

    es = ExitStack()
    S = lambda name: Sem(es, nc, name)
    sX = S("sX"); sXr = S("sXr"); sW = S("sW"); sCast = S("sCast")
    sLN = S("sLN"); sTP = S("sTP"); sFM = S("sFM")
    sQKV = S("sQKV"); sEv = S("sEv")
    sBnG = S("sBnG"); sMV = S("sMV"); sIstd = S("sIstd")
    sBs = S("sBs"); sBt = S("sBt"); sBv = S("sBv"); sBq = S("sBq")
    sS = S("sS"); sMsk = S("sMsk"); sPT = S("sPT"); sPV = S("sPV")
    sLrow = S("sLrow"); sLt = S("sLt"); sRc = S("sRc")
    sOT = S("sOT"); sOP = S("sOP"); sY = S("sY"); sX1 = S("sX1"); sSp = S("sSp")
    sTd = S("sTd"); sVd = S("sVd"); sSd = S("sSd")
    sWB = S("sWB"); sCastB = S("sCastB"); sXnB = S("sXnB"); sX1B = S("sX1B")
    sG = S("sG"); sSil = S("sSil"); sU = S("sU"); sHT = S("sHT")
    sDnT = S("sDnT"); sPw = S("sPw"); sEpB = S("sEpB"); sX1Bt = S("sX1Bt")
    sOut = S("sOut"); sSc = S("sSc"); sVc = S("sVc"); sGq = S("sGq"); sWB2 = S("sWB2")

    pO01 = es.enter_context(nc.psum_tensor("pO01", [128, 512], F32))
    pO23 = es.enter_context(nc.psum_tensor("pO23", [128, 512], F32))
    pAux2 = es.enter_context(nc.psum_tensor("pAux2", [128, 512], F32))
    pAux3 = es.enter_context(nc.psum_tensor("pAux3", [128, 512], F32))
    pS0 = es.enter_context(nc.psum_tensor("pS0", [128, 512], F32))
    pS1 = es.enter_context(nc.psum_tensor("pS1", [128, 512], F32))

    qs = {k: EngQ() for k in ("sync", "tensor", "vector", "scalar", "gpsimd")}

    def W(eng, sem, val):
        if val > 0:
            qs[eng].ops.append(lambda e, s=sem.h, v=val: e.wait_ge(s, v))

    def OP(eng, fn, sem=None, k=1):
        if sem is None:
            qs[eng].ops.append(fn)
            return None
        sem.n += k
        qs[eng].ops.append(lambda e, f=fn, s=sem.h, kk=k: f(e).then_inc(s, kk))
        return sem.n

    esA = ExitStack()
    sb = lambda name, shape, dt: esA.enter_context(nc.sbuf_tensor(name, shape, dt))
    K_s = [sb(f"K{c}", [128, 8192], BF16) for c in range(4)]
    V_s = sb("V", [128, 64 * 512], BF16)
    Wq_s = sb("Wq", [128, 8 * 1536], BF16)
    Wo_s = sb("Wo", [128, 4 * 1024], BF16)
    bout_s = sb("bout", [128, 1024], BF16)
    mask2_s = sb("m2", [128, 256], BF16)
    mask3_s = sb("m3", [128, 256], BF16)
    pc01_s = sb("pc01", [128, 1], F32)
    ident_s = sb("ident", [128, 128], BF16)
    onesc_s = sb("onesc", [128, 1], BF16)
    onesr_s = sb("onesr", [1, 128], F32)
    onef_s = sb("onef", [1, 1], F32)
    econst_s = sb("econst", [128, 2], F32)
    xrot = [sb(f"xr{i}", [128, 1024], F32) for i in range(2)]
    xres_s = sb("xres", [128, 1024], F32)
    xn_s = [sb(f"xn{i}", [128, 1024], BF16) for i in range(2)]
    junk_s = xn_s[0]
    xnfm_s = sb("xnfm", [128, 8 * 256], BF16)
    Qfm_s = sb("Qfm", [128, 4 * 256], BF16)
    PT_s = [sb(f"PT{i}", [128, 256], BF16) for i in range(2)]
    OT_s = sb("OT", [128, 4 * 256], BF16)
    linv_s = sb("linv", [128, 2], F32)
    rows_s = sb("rows", [1, 520], F32)
    bias_s = sb("bias", [128, 4], F32)
    stats_s = sb("stats", [128, 12], F32)
    mv_s = sb("mv", [128, 8], F32)
    istd_s = sb("istd", [128, 8], F32)
    y2_s = sb("y2", [128, 2048], BF16)
    x1o_s = sb("x1o", [128, 1024], F32)
    xnb1_tm = [sb(f"xb1t{i}", [128, 1024], BF16) for i in range(2)]
    xnb1_s = sb("xb1s", [128, 8 * 256], BF16)
    pT = [esA.enter_context(nc.psum_tensor(f"pT{i}", [128, 1024], BF16)) for i in range(2)]

    for dst, src in ((mask2_s, mask2d), (mask3_s, mask3d), (pc01_s, pc01d),
                     (ident_s, identd), (onesc_s, onescd), (onesr_s, onesrd),
                     (onef_s, onefd), (econst_s, econstd)):
        OP("sync", lambda e, d=dst, s=src: e.dma_start(out=d[:, :], in_=s), sW, 16)
    OP("sync", lambda e: e.dma_start(out=x1o_s[:, :], in_=boutb), sW, 16)
    W("scalar", sW, sW.n)
    OP("scalar", lambda e: e.copy(bout_s[:, :], x1o_s[:, :]), sCast, 1)
    stage = [xrot[0], x1o_s]
    si = 0
    for dc in range(8):
        for (c0, w_) in ((0, 1024), (1024, 512)):
            buf = stage[si % 2]
            W("sync", sCast, sCast.n)
            OP("sync", lambda e, b=buf, d=dc, c=c0, w=w_: e.dma_start(
                out=b[:, 0:w], in_=wqkvT[128 * d:128 * d + 128, c:c + w]), sW, 16)
            W("sync", sW, sW.n)
            W("scalar", sW, sW.n)
            OP("scalar", lambda e, b=buf, d=dc, c=c0, w=w_: e.copy(
                Wq_s[:, 1536 * d + c:1536 * d + c + w], b[:, 0:w]), sCast, 1)
            si += 1
    for vc in range(4):
        buf = stage[si % 2]
        W("sync", sCast, sCast.n)
        OP("sync", lambda e, b=buf, v=vc: e.dma_start(
            out=b[:, :], in_=woutT[128 * v:128 * v + 128, :]), sW, 16)
        W("sync", sW, sW.n)
        W("scalar", sW, sW.n)
        OP("scalar", lambda e, b=buf, v=vc: e.copy(
            Wo_s[:, 1024 * v:1024 * v + 1024], b[:, :]), sCast, 1)
        si += 1
    castA_done = sCast.n
    OP("vector", lambda e: e.memset(rows_s[0:1, 514:515], 0.0), sBv, 1)

    ln_cnt = {}
    x_cnt = {}
    expc = {}
    pvc = {}
    spill_after = [0] * (NBLK + 1)
    x1tt = {}
    fm_evac_at = {0: 0, 1: 0}
    ev_at = {0: 0, 1: 0}
    sy_step = [0] * (NBLK + 1)
    st = {"flash_s_prev": 0, "sq_done_prev": 0, "prev_half_qkv": 0}
    tp_xn_at = {0: 0, 1: 0}
    xb1tp_at = {0: 0, 1: 0}
    nb_d = {}
    gp1_d = {}
    gpk_d = {}
    xn0r_d = {}
    xn1r_d = {}
    gpk23_d = {}
    bm_st = {}
    qmx_d = {}
    bn_d = {}
    x1sp = {}
    xnsp = {}
    yfree_d = {}
    istd2_d = {}
    istdb_d = {}
    xr_d = {}
    qkv_ev_d = {}
    bq_d = {}
    flash_done_d = {}
    rc_d = {}

    def stage_LN1(j, subs=(0, 1, 2, 3)):
        for s in subs:
            b = s % 2
            c0 = 2 * b
            prev = ln_cnt.get((j, s - 2), ln_cnt.get((j - 1, s + 2), 0))
            if prev > 0:
                W("sync", sLN, prev)
            if j == 0:
                W("sync", sCast, castA_done)
            r0 = 512 * j + 128 * s
            x_cnt[(j, s)] = OP("sync", lambda e, r=r0, xb=xrot[b]: e.dma_start(
                out=xb[:, :], in_=xc[r:r + 128, :]), sX, 16)
            W("sync", sX, x_cnt[(j, s)])
            W("vector", sX, x_cnt[(j, s)])
            W("vector", sMV, sMV.n)
            OP("vector", lambda e, xb=xrot[b]: e.bn_stats(stats_s[:, 0:6], xb[:, 0:512]), sBnG, 1)
            g = OP("vector", lambda e, xb=xrot[b]: e.bn_stats(stats_s[:, 6:12], xb[:, 512:1024]), sBnG, 1)
            W("vector", sBnG, g)
            mvc = OP("vector", lambda e, c=c0: e.bn_aggr(mv_s[:, c:c + 2], stats_s[:, :]), sMV, 1)
            W("scalar", sMV, mvc)
            lnch = OP("scalar", lambda e, c=c0: e.activation(
                out=istd_s[:, c + 1:c + 2], in_=mv_s[:, c + 1:c + 2],
                func=AF.Ln, bias=econst_s[:, 0:1], scale=1.0), sSc, 1)
            W("scalar", sSc, lnch)
            ic = OP("scalar", lambda e, c=c0: e.activation(
                out=istd_s[:, c:c + 1], in_=istd_s[:, c + 1:c + 2],
                func=AF.Exp, scale=-0.5), sIstd, 1)
            W("vector", sIstd, ic)
            if s == 0 and j > 0:
                W("vector", sGq, gp1_d.get(j - 1, 0))
                W("vector", sBt, xn0r_d.get(j - 1, 0))
            if s == 1 and j > 0:
                W("vector", sGq, gpk_d.get(j - 1, 0))
                W("vector", sBt, xn1r_d.get(j - 1, 0))
            W("vector", sTP, tp_xn_at[b])
            ln_cnt[(j, s)] = OP("vector", lambda e, xb=xrot[b], c=c0, o=xn_s[b]: e.tensor_scalar(
                out=o[:, :], in0=xb[:, :], scalar1=mv_s[:, c:c + 1], scalar2=istd_s[:, c:c + 1],
                op0=ALU.subtract, op1=ALU.mult), sLN, 1)

    def stage_qkv(j):
        for s in range(4):
            b = s % 2
            if s == 2:
                stage_LN1(j, (2, 3))
            W("tensor", sLN, ln_cnt[(j, s)])
            W("tensor", sFM, fm_evac_at[b])
            for c in range(8):
                fn = lambda e, cc=c, xb=xn_s[b], pb=pT[b]: e.transpose(
                    pb[:, 128 * cc:128 * cc + 128], xb[:, 128 * cc:128 * cc + 128], ident_s[:, :])
                tpc = OP("tensor", fn, sTP, 1) if c == 7 else OP("tensor", fn)
            tp_xn_at[b] = tpc
            W("scalar", sTP, tpc)
            if j == 0 and s == 0:
                W("scalar", sCast, castA_done)
            W("scalar", sQKV, st["prev_half_qkv"])
            u = s % 2
            src = pT[b].rearrange("p (c w) -> p c w", w=128)
            dst = xnfm_s.rearrange("p (c w) -> p c w", w=256)[:, :, 128 * u:128 * u + 128]
            fm_evac_at[b] = OP("scalar", lambda e, d=dst, sr=src: e.copy(d, sr), sFM, 1)
            drain_epi(1)

            if s in (1, 3):
                h = s // 2
                W("tensor", sFM, fm_evac_at[0])
                W("tensor", sFM, fm_evac_at[1])
                if j == 0 and h == 0:
                    W("tensor", sCast, castA_done)
                if h == 0:
                    W("tensor", sY, sy_step[j - 1] if j > 0 else 0)
                groups = [("K", kc) for kc in range(4)] + [("V", v) for v in range(2)]
                if h == 1:
                    groups += [("Q", qc) for qc in range(4)]
                for gi, (kind, idx) in enumerate(groups):
                    bank = (pS0, pS1)[gi % 2]
                    W("tensor", sEv, ev_at[gi % 2])
                    for dc in range(8):
                        if kind == "K":
                            fn = lambda e, d=dc, k=idx, bk=bank: e.matmul(
                                bk[:, 0:256],
                                lhsT=Wq_s[:, 1536 * d + 512 + 128 * k:1536 * d + 512 + 128 * k + 128],
                                rhs=xnfm_s[:, 256 * d:256 * d + 256], start=d == 0, stop=d == 7)
                        elif kind == "V":
                            fn = lambda e, d=dc, v=idx, bk=bank: e.matmul(
                                bk[:, 0:512],
                                lhsT=xnfm_s[:, 256 * d + 128 * v:256 * d + 128 * v + 128],
                                rhs=Wq_s[:, 1536 * d + 1024:1536 * d + 1536], start=d == 0, stop=d == 7)
                        else:
                            fn = lambda e, d=dc, q=idx, bk=bank: e.matmul(
                                bk[:, 0:256],
                                lhsT=Wq_s[:, 1536 * d + 128 * q:1536 * d + 128 * q + 128],
                                rhs=xnfm_s[:, 256 * d:256 * d + 256], start=d == 0, stop=d == 7)
                        qk = OP("tensor", fn, sQKV, 1) if dc == 7 else OP("tensor", fn)
                    W("scalar", sQKV, qk)
                    if kind == "K":
                        col = 512 * j + 256 * h
                        evc = OP("scalar", lambda e, k=idx, cl=col, bk=bank: e.copy(
                            K_s[k][:, cl:cl + 256], bk[:, 0:256]), sEv, 1)
                    elif kind == "V":
                        t = 4 * j + 2 * h + idx
                        evc = OP("scalar", lambda e, t_=t, bk=bank: e.copy(
                            V_s[:, 512 * t_:512 * t_ + 512], bk[:, 0:512]), sEv, 1)
                    else:
                        if idx == 0:
                            W("scalar", sS, st["flash_s_prev"])
                            W("scalar", sGq, gp1_d.get(j - 1, 0))
                        evc = OP("scalar", lambda e, q=idx, bk=bank: e.copy(
                            Qfm_s[:, 256 * q:256 * q + 256], bk[:, 0:256]), sEv, 1)
                    ev_at[gi % 2] = evc
                st["prev_half_qkv"] = sQKV.n
        qkv_ev_d[j] = sEv.n

    def stage_bmach_gp(j):
        W("gpsimd", sTP, tp_xn_at[0])
        W("gpsimd", sLN, ln_cnt[(j, 2)])
        W("gpsimd", sEv, qkv_ev_d[j])
        gp1_d[j] = OP("gpsimd", lambda e: e.tensor_tensor(
            out=junk_s[:, :], in0=Qfm_s[:, :], in1=Qfm_s[:, :], op=ALU.mult), sGq, 1)
        W("gpsimd", sTP, tp_xn_at[1])
        W("gpsimd", sLN, ln_cnt[(j, 3)])
        col = 512 * j
        for i in range(2):
            gpk = OP("gpsimd", lambda e, i_=i, cl=col: e.tensor_tensor(
                out=xn_s[1][:, 512 * i_:512 * i_ + 512], in0=K_s[i_][:, cl:cl + 512],
                in1=K_s[i_][:, cl:cl + 512], op=ALU.mult), sGq, 1)
        gpk_d[j] = gpk
        W("gpsimd", sOP, op_d.get((j - 1, 1, 1), 0))
        W("gpsimd", sOT, 2 * j)
        for i in range(2):
            gpk23 = OP("gpsimd", lambda e, i_=i, cl=col: e.tensor_tensor(
                out=OT_s[:, 512 * i_:512 * i_ + 512], in0=K_s[2 + i_][:, cl:cl + 512],
                in1=K_s[2 + i_][:, cl:cl + 512], op=ALU.mult), sGq, 1)
        gpk23_d[j] = gpk23

    def stage_bmach_mm1(j):
        W("tensor", sGq, gp1_d[j])
        if j > 0:
            W("tensor", sLrow, j + 1)
        for c in range(4):
            qmm = OP("tensor", lambda e, cc=c: e.matmul(
                pAux2[0:1, 0:256], lhsT=onesc_s[:, :],
                rhs=junk_s[:, 256 * cc:256 * cc + 256],
                start=cc == 0, stop=cc == 3), sBt, 1)
        xn0r_d[j] = qmm
        W("tensor", sGq, gpk_d[j])
        W("tensor", sBs, bn_d.get(j - 1, 0))
        for i in range(2):
            OP("tensor", lambda e, i_=i: e.matmul(
                pAux3[0:1, 0:512], lhsT=onesc_s[:, :],
                rhs=xn_s[1][:, 512 * i_:512 * i_ + 512],
                start=i_ == 0, stop=False), sBt, 1)
        W("tensor", sGq, gpk23_d[j])
        for i in range(2):
            kmm = OP("tensor", lambda e, i_=i: e.matmul(
                pAux3[0:1, 0:512], lhsT=onesc_s[:, :],
                rhs=OT_s[:, 512 * i_:512 * i_ + 512],
                start=False, stop=i_ == 1), sBt, 1)
        xn1r_d[j] = kmm
        W("vector", sBt, qmm)
        qmx = OP("vector", lambda e: e.reduce_max(rows_s[0:1, 512:513], pAux2[0:1, 0:256],
                                                  axis=AX.X), sBv, 1)
        qmx_d[j] = qmx
        W("vector", sBt, kmm)
        kb = OP("vector", lambda e: e.reduce_max(rows_s[0:1, 513:514], pAux3[0:1, 0:512],
                                                 axis=AX.X), sBv, 1)
        W("vector", sBv, kb)
        kmx = OP("vector", lambda e: e.tensor_tensor(out=rows_s[0:1, 514:515], in0=rows_s[0:1, 513:514],
                                                     in1=rows_s[0:1, 514:515], op=ALU.max), sBv, 1)
        W("vector", sBv, kmx)
        bv = OP("vector", lambda e: e.tensor_tensor(out=rows_s[0:1, 515:516], in0=rows_s[0:1, 512:513],
                                                    in1=rows_s[0:1, 514:515], op=ALU.mult), sBv, 1)
        bm_st[j] = bv

    def stage_bmach_mm2(j):
        bv = bm_st[j]
        W("scalar", sBv, bv)
        lnch = OP("scalar", lambda e: e.activation(out=rows_s[0:1, 516:517], in_=rows_s[0:1, 515:516],
                                                   func=AF.Ln, scale=1.0), sSc, 1)
        W("scalar", sSc, lnch)
        nb_d[j] = OP("scalar", lambda e: e.activation(out=rows_s[0:1, 517:518], in_=rows_s[0:1, 516:517],
                                                      func=AF.Exp, bias=econst_s[0:1, 1:2], scale=0.5), sBs, 1)

    def stage_bmachB2(j):
        W("tensor", sBs, nb_d[j])
        bc = OP("tensor", lambda e: e.matmul(pAux3[:, 0:1], lhsT=onesr_s[0:1, :],
                                             rhs=rows_s[0:1, 517:518], start=True, stop=True), sBt, 1)
        W("scalar", sBt, bc)
        bsl = 2 * (j % 2)
        bn = OP("scalar", lambda e, b0=bsl: e.copy(bias_s[:, b0:b0 + 1], pAux3[:, 0:1]), sBs, 1)
        bn_d[j] = bn
        W("vector", sBs, bn)
        bq_d[j] = OP("vector", lambda e, b0=bsl: e.tensor_tensor(out=bias_s[:, b0 + 1:b0 + 2],
                                                                 in0=bias_s[:, b0:b0 + 1],
                                                                 in1=pc01_s[:, :], op=ALU.add), sBq, 1)

    def stage_flash(j):
        ntile = 4 * j + 4
        W("scalar", sBq, max(j, 1))

        def emit_S(t):
            bank = (pS0, pS1)[t % 2]
            if t >= 2:
                W("tensor", sPT, expc[(j, t - 2)])
            else:
                W("tensor", sEv, qkv_ev_d[j])
            for c in range(4):
                fn = lambda e, cc=c, t_=t, bk=bank: e.matmul(
                    bk[:, 0:256], lhsT=K_s[cc][:, 128 * t_:128 * t_ + 128],
                    rhs=Qfm_s[:, 256 * cc:256 * cc + 256], start=cc == 0, stop=cc == 3)
                sg = OP("tensor", fn, sS, 1) if c == 3 else OP("tensor", fn)
            diag = t >= 4 * j + 2
            if diag:
                W("vector", sS, sg)
                msk = mask2_s if t == 4 * j + 2 else mask3_s
                mk = OP("vector", lambda e, bk=bank, m=msk: e.tensor_tensor(
                    out=bk[:, 0:256], in0=bk[:, 0:256], in1=m[:, :], op=ALU.add), sMsk, 1)
                W("scalar", sMsk, mk)
            else:
                W("scalar", sS, sg)
            if t >= 2:
                W("scalar", sPV, pvc[(j, t - 2)])
            bp = 2 * (max(j - 1, 0) % 2)
            bias_ap = bias_s[:, bp + 1:bp + 2] if (4 * j <= t < 4 * j + 2) else bias_s[:, bp:bp + 1]
            expc[(j, t)] = OP("scalar", lambda e, bk=bank, ba=bias_ap, pt=PT_s[t % 2]: e.activation(
                out=pt[:, :], in_=bk[:, 0:256], func=AF.Exp, bias=ba, scale=SCALE), sPT, 1)

        def emit_PV(t):
            W("tensor", sPT, expc[(j, t)])
            if t == 0:
                W("tensor", sY, sy_step[j - 1] if j > 0 else 0)
                W("tensor", sLrow, j)
                W("tensor", sBv, qmx_d.get(j - 1, qmx_d.get(0, 0)))
                W("tensor", sRc, rc_d.get(j - 1, 0))
            first = t == 0
            lastt = t == ntile - 1
            for c in range(4):
                pb = pO01 if c < 2 else pO23
                cc0 = 256 * (c % 2)
                stf = first and (c % 2 == 0)
                OP("tensor", lambda e, t_=t, cc=c, pb_=pb, c0=cc0, st_=stf, sp_=lastt: e.matmul(
                    pb_[:, c0:c0 + 256], lhsT=V_s[:, 512 * t_ + 128 * cc:512 * t_ + 128 * cc + 128],
                    rhs=PT_s[t_ % 2][:, :], start=st_, stop=sp_, skip_group_check=True))
            pvc[(j, t)] = OP("tensor", lambda e, t_=t, st_=first, sp_=lastt: e.matmul(
                pAux2[0:1, 0:256], lhsT=onesc_s[:, :], rhs=PT_s[t_ % 2][:, :],
                start=st_, stop=sp_), sPV, 1)

        for t in range(ntile):
            emit_S(t)
            if t >= 1:
                emit_PV(t - 1)
        emit_PV(ntile - 1)
        st["flash_s_prev"] = sS.n
        flash_done_d[j] = pvc[(j, ntile - 1)]

    def stage_out(j):
        ntile = 4 * j + 4
        flash_done = flash_done_d[j]
        W("scalar", sPV, flash_done)
        W("scalar", sLt, 2 * j)
        lr = OP("scalar", lambda e: e.copy(rows_s[0:1, 0:256], pAux2[0:1, 0:256]), sLrow, 1)
        W("tensor", sLrow, lr)
        W("tensor", sBv, qmx_d.get(j, 0))
        W("tensor", sRc, rc_d.get(j - 1, 0))
        for u in range(2):
            lt = OP("tensor", lambda e, u_=u: e.matmul(
                pAux2[:, 256 + u_:257 + u_], lhsT=rows_s[0:1, 128 * u_:128 * u_ + 128],
                rhs=onef_s[0:1, 0:1], start=True, stop=True), sLt, 1)
        W("vector", sLt, lt)
        rc_d[j] = OP("vector", lambda e: e.reciprocal(linv_s[:, 0:2], pAux2[:, 256:258]), sRc, 1)
        W("scalar", sBt, xn1r_d.get(j, 0))
        OP("scalar", lambda e: e.copy(OT_s[:, 0:512], pO01[:, :]), sOT, 1)
        W("vector", sPV, flash_done)
        W("vector", sBt, xn1r_d.get(j, 0))
        ot = OP("vector", lambda e: e.tensor_copy(OT_s[:, 512:1024], pO23[:, :]), sOT, 1)
        for u in range(2):
            for hh in range(2):
                pb = (pO01, pO23)[hh] if u == 0 else (pS0, pS1)[hh]
                W("tensor", sOT, ot)
                if u == 1 and hh == 0:
                    W("tensor", sPT, expc[(j, ntile - 1)])
                for c in range(4):
                    fn = lambda e, cc=c, u_=u, h_=hh, pb_=pb: e.matmul(
                        pb_[:, 0:512], lhsT=OT_s[:, 256 * cc + 128 * u_:256 * cc + 128 * u_ + 128],
                        rhs=Wo_s[:, 1024 * cc + 512 * h_:1024 * cc + 512 * h_ + 512],
                        start=cc == 0, stop=cc == 3)
                    opc = OP("tensor", fn, sOP, 1) if c == 3 else OP("tensor", fn)
                op_d[(j, u, hh)] = opc

    def stage_epi_stt(j):
        for u in range(2):
            for hh in range(2):
                pb = (pO01, pO23)[hh] if u == 0 else (pS0, pS1)[hh]
                W("vector", sOP, op_d[(j, u, hh)])
                if u == 0 and hh == 0:
                    W("vector", sRc, rc_d[j])
                    if j == 0:
                        W("vector", sCast, castA_done)
                    W("vector", sLN, yfree_d.get(j - 1, 0))
                OP("vector", lambda e, u_=u, h_=hh, pb_=pb: e.scalar_tensor_tensor(
                    out=y2_s[:, 1024 * u_ + 512 * h_:1024 * u_ + 512 * h_ + 512], in0=pb_[:, 0:512],
                    scalar=linv_s[:, u_:u_ + 1], in1=bout_s[:, 512 * h_:512 * h_ + 512],
                    op0=ALU.mult, op1=ALU.add), sY, 1)
        sy_step[j] = sY.n

    epi_pend = []

    def drain_epi(n):
        k = 0
        while epi_pend and k < n:
            epi_pend.pop(0)()
            k += 1

    def stage_epi_prepare(j):
        def ln2_piece(u):
            def f():
                W("vector", sY, sy_step[j])
                W("vector", sMV, sMV.n)
                OP("vector", lambda e, u_=u: e.bn_stats(stats_s[:, 0:6], y2_s[:, 1024 * u_:1024 * u_ + 512]), sBnG, 1)
                g = OP("vector", lambda e, u_=u: e.bn_stats(stats_s[:, 6:12], y2_s[:, 1024 * u_ + 512:1024 * u_ + 1024]), sBnG, 1)
                W("vector", sBnG, g)
                mvc = OP("vector", lambda e: e.bn_aggr(mv_s[:, 4:6], stats_s[:, :]), sMV, 1)
                W("scalar", sMV, mvc)
                lnch = OP("scalar", lambda e: e.activation(out=istd_s[:, 5:6], in_=mv_s[:, 5:6],
                                                    func=AF.Ln, bias=econst_s[:, 0:1], scale=1.0), sSc, 1)
                W("scalar", sSc, lnch)
                ic = OP("scalar", lambda e: e.activation(out=istd_s[:, 4:5], in_=istd_s[:, 5:6],
                                                         func=AF.Exp, scale=-0.5), sIstd, 1)
                istd2_d[(j, u)] = ic
            return f

        def x1_piece(u):
            def f():
                W("vector", sIstd, istd2_d[(j, u)])
                prevsp = x1sp.get((j, u - 1), x1sp.get((j - 1, 1), 0))
                W("vector", sSp, prevsp)
                vch = OP("vector", lambda e, u_=u: e.tensor_scalar(
                    out=x1o_s[:, :], in0=y2_s[:, 1024 * u_:1024 * u_ + 1024], scalar1=mv_s[:, 4:5],
                    scalar2=istd_s[:, 4:5], op0=ALU.subtract, op1=ALU.mult), sVc, 1)
                W("vector", sVc, vch)
                W("vector", sXr, xr_d[(j, u)])
                x1tt[(j, u)] = OP("vector", lambda e: e.tensor_tensor(
                    out=x1o_s[:, :], in0=x1o_s[:, :], in1=xres_s[:, :], op=ALU.add), sX1, 1)
                W("vector", sX1, x1tt[(j, u)])
                W("vector", sMV, sMV.n)
                OP("vector", lambda e: e.bn_stats(stats_s[:, 0:6], x1o_s[:, 0:512]), sBnG, 1)
                g = OP("vector", lambda e: e.bn_stats(stats_s[:, 6:12], x1o_s[:, 512:1024]), sBnG, 1)
                W("vector", sBnG, g)
                mvc = OP("vector", lambda e: e.bn_aggr(mv_s[:, 6:8], stats_s[:, :]), sMV, 1)
                W("scalar", sMV, mvc)
                lnch = OP("scalar", lambda e: e.activation(out=istd_s[:, 7:8], in_=mv_s[:, 7:8],
                                                    func=AF.Ln, bias=econst_s[:, 0:1], scale=1.0), sSc, 1)
                W("scalar", sSc, lnch)
                ic = OP("scalar", lambda e: e.activation(out=istd_s[:, 6:7], in_=istd_s[:, 7:8],
                                                         func=AF.Exp, scale=-0.5), sIstd, 1)
                istdb_d[(j, u)] = ic
            return f

        def norm_piece(u):
            def f():
                W("vector", sIstd, istdb_d[(j, u)])
                W("vector", sTP, xb1tp_at[u])
                ln_cnt[(j, 4 + u)] = OP("vector", lambda e, u_=u: e.tensor_scalar(
                    out=xnb1_tm[u_][:, :], in0=x1o_s[:, :], scalar1=mv_s[:, 6:7], scalar2=istd_s[:, 6:7],
                    op0=ALU.subtract, op1=ALU.mult), sLN, 1)
                if u == 1:
                    yfree_d[j] = ln_cnt[(j, 5)]
                W("gpsimd", sX1, x1tt[(j, u)])
                r0 = 256 * j + 128 * u
                x1sp[(j, u)] = OP("gpsimd", lambda e, r=r0: e.dma_start(out=x1s[r:r + 128, :], in_=x1o_s[:, :]), sSp, 16)
                W("gpsimd", sSp, sSp.n)
                if u == 0:
                    r1 = 512 * j + 256 + 128
                elif j + 1 < NBLK:
                    r1 = 512 * (j + 1) + 256
                else:
                    r1 = None
                if r1 is not None:
                    OP("gpsimd", lambda e, r=r1: e.dma_start(out=xres_s[:, :], in_=xc[r:r + 128, :]), sXr, 16)
                    W("gpsimd", sXr, sXr.n)
                    xr_d[(j + u, 1 - u)] = sXr.n
            return f

        for u in range(2):
            epi_pend.append(ln2_piece(u))
            epi_pend.append(x1_piece(u))
            epi_pend.append(norm_piece(u))

    def stage_xnb1(j):
        for u in range(2):
            b = u % 2
            W("tensor", sLN, ln_cnt[(j, 4 + u)])
            W("tensor", sFM, fm_evac_at[b])
            for c in range(8):
                fn = lambda e, cc=c, u_=u, pb=pT[b]: e.transpose(
                    pb[:, 128 * cc:128 * cc + 128], xnb1_tm[u_][:, 128 * cc:128 * cc + 128], ident_s[:, :])
                tpb = OP("tensor", fn, sTP, 1) if c == 7 else OP("tensor", fn)
            xb1tp_at[u] = tpb
            W("scalar", sTP, tpb)
            W("scalar", sSp, xnsp.get(j - 1, 0))
            src = pT[b].rearrange("p (c w) -> p c w", w=128)
            dst = xnb1_s.rearrange("p (c w) -> p c w", w=256)[:, :, 128 * u:128 * u + 128]
            fm_evac_at[b] = OP("scalar", lambda e, d=dst, sr=src: e.copy(d, sr), sFM, 1)
        W("gpsimd", sFM, fm_evac_at[1])
        for c in range(8):
            OP("gpsimd", lambda e, cc=c, jj=j: e.dma_start(
                out=xnfms[cc, :, 256 * jj:256 * jj + 256], in_=xnb1_s[:, 256 * cc:256 * cc + 256]), sSp, 16)
        xnsp[j] = sSp.n
        W("gpsimd", sSp, xnsp[j])

    op_d = {}
    OP("gpsimd", lambda e: e.dma_start(out=xres_s[:, :], in_=xc[256:384, :]), sXr, 16)
    W("gpsimd", sXr, sXr.n)
    xr_d[(0, 0)] = sXr.n
    stage_LN1(0, (0, 1))
    stage_qkv(0)
    drain_epi(99)
    stage_bmach_gp(0)
    stage_bmach_mm1(0)
    stage_bmach_mm2(0)
    stage_bmachB2(0)
    for j in range(NBLK):
        if j >= 2:
            stage_bmachB2(j - 1)
        if 1 <= j < NBLK - 1:
            stage_bmach_gp(j)
        stage_flash(j)
        if 1 <= j < NBLK - 1:
            stage_bmach_mm1(j)
        stage_out(j)
        if 1 <= j < NBLK - 1:
            stage_bmach_mm2(j)
        if j + 1 < NBLK:
            stage_LN1(j + 1, (0, 1))
        stage_epi_stt(j)
        stage_epi_prepare(j)
        if j >= 1:
            stage_xnb1(j - 1)
        if j + 1 < NBLK:
            stage_qkv(j + 1)
        drain_epi(99)
    stage_xnb1(NBLK - 1)

    tD = OP("tensor", lambda e: e.nop(), sTd, 1)
    vD = OP("vector", lambda e: e.nop(), sVd, 1)
    scD = OP("scalar", lambda e: e.nop(), sSd, 1)
    esA.close()

    esB = ExitStack()
    sbB = lambda name, shape, dt: esB.enter_context(nc.sbuf_tensor(name, shape, dt))
    Wg_s = sbB("Wg", [128, HC * 1024], BF16)
    Wu_s = sbB("Wu", [128, HC * 1024], BF16)
    Wd_s = sbB("Wd", [128, HC * 1024], BF16)
    wstgB = [sbB(f"wstg{i}", [128, 1024], I32) for i in range(4)]
    xnfmB = [sbB(f"xnB{i}", [128, 8 * 256], BF16) for i in range(2)]
    x1B = [sbB(f"x1B{i}", [128, 2048], F32) for i in range(2)]
    gsilu = [sbB(f"gs{i}", [128, 256], BF16) for i in range(2)]
    hT_s = [sbB(f"hT{i}", [128, 256], BF16) for i in range(2)]
    outB = [sbB(f"oB{i}", [128, 1024], F32) for i in range(2)]
    y3B = [sbB(f"y3B{i}", [128, 1024], F32) for i in range(2)]
    statsB = sbB("stB", [128, 12], F32)
    mvB = sbB("mvB", [128, 4], F32)
    vpe_s = sbB("vpe", [128, 2], F32)
    istdB = sbB("isB", [128, 2], F32)
    nhalf = sbB("nh", [128, 1], F32)
    pB6 = esB.enter_context(nc.psum_tensor("pB6", [128, 512], F32))
    pB7 = esB.enter_context(nc.psum_tensor("pB7", [128, 512], F32))
    pDn = [pO01, pO23, pAux2, pAux3]
    pG = [pS0, pB6]
    pU = [pS1, pB7]

    _all_sems = [sX, sXr, sW, sCast, sLN, sTP, sFM, sQKV, sEv, sBnG, sMV, sIstd,
                 sBs, sBt, sBv, sBq, sS, sMsk, sPT, sPV, sLrow, sLt, sRc, sOT,
                 sOP, sY, sX1, sSp, sSc, sVc, sGq, sTd, sVd, sSd]
    for e_ in ("sync", "gpsimd", "scalar", "vector", "tensor"):
        for _sm in _all_sems:
            W(e_, _sm, _sm.n)
    W("sync", sSp, xnsp[NBLK - 1])
    OP("gpsimd", lambda e: e.memset(nhalf[:, :], -0.5), sPw, 1)

    castB = {}
    wi = [0]

    wiC = [0]

    def issue_w(w, hc):
        k = wi[0]
        ring = "sync" if k % 2 == 0 else "gpsimd"
        sem = sWB if k % 2 == 0 else sWB2
        buf = wstgB[k % 4]
        W(ring, sCastB, max(0, k - 3))
        W(ring, sem, 16 * (k // 2))
        src = (wgT, wuT, wdT)[w]
        OP(ring, lambda e, b=buf, s=src, h=hc: e.dma_start(out=b[:, :], in_=s[h, :, :]), sem, 16)
        wi[0] += 1

    def cast_w(w, hc):
        k = wiC[0]
        buf = wstgB[k % 4]
        W("scalar", sWB if k % 2 == 0 else sWB2, 16 * (k // 2 + 1))
        dstw = (Wg_s, Wu_s, Wd_s)[w]
        castB[(w, hc)] = OP("scalar", lambda e, b=buf, d=dstw, h=hc: e.activation(
            out=d[:, 1024 * h:1024 * h + 1024], in_=b[:, :], func=AF.Copy, bias=0.0, scale=QDEQ),
            sCastB, 1)
        wiC[0] += 1

    xn_ld = [0] * NBLK
    x1_ld = [0] * NBLK
    y3_read = {}
    ln3_pend = {}

    def emit_ln3_tail(g):
        pw0, pw1 = ln3_pend.pop(g)
        for u in range(2):
            W("vector", sPw, (pw0, pw1)[u])
            W("vector", sOut, st_cnt.get((g - 1, u), 0))
            for hh in range(2):
                y3v = OP("vector", lambda e, u_=u, h_=hh: e.tensor_scalar(
                    out=outB[u_][:, 512 * h_:512 * h_ + 512], in0=y3B[u_][:, 512 * h_:512 * h_ + 512],
                    scalar1=mvB[:, 2 * u_:2 * u_ + 1], scalar2=istdB[:, u_:u_ + 1],
                    op0=ALU.subtract, op1=ALU.mult), sVc, 1)
            y3_read[g] = y3v
            W("vector", sVc, y3v)
            W("vector", sX1B, x1_ld[g])
            tt = OP("vector", lambda e, u_=u, gg=g: e.tensor_tensor(
                out=outB[u_][:, :], in0=outB[u_][:, :], in1=x1B[gg % 2][:, 1024 * u_:1024 * u_ + 1024],
                op=ALU.add), sX1Bt, 1)
            W("gpsimd", sX1Bt, tt)
            r0 = 256 * g + 128 * u
            st_cnt[(g, u)] = OP("gpsimd", lambda e, r=r0, u_=u: e.dma_start(
                out=out_c[r:r + 128, :], in_=outB[u_][:, :]), sOut, 16)
            W("gpsimd", sOut, st_cnt[(g, u)])
    up_cnt = {}
    sil_at = {0: 0, 1: 0}
    ht_at = {0: 0, 1: 0}
    ep_cnt = [0] * (NBLK + 1)
    st_cnt = {}

    def emit_loads(g):
        if g >= 2:
            W("sync", sU, up_cnt[(g - 2, HC - 1)])
        for c in range(8):
            OP("sync", lambda e, cc=c, gg=g: e.dma_start(
                out=xnfmB[gg % 2][:, 256 * cc:256 * cc + 256],
                in_=xnfms[cc, :, 256 * gg:256 * gg + 256]), sXnB, 16)
        xn_ld[g] = sXnB.n
        W("sync", sXnB, xn_ld[g])
        W("sync", sSp, xnsp[g])
        if g >= 2:
            W("sync", sX1Bt, 2 * (g - 1))
        for u in range(2):
            r0 = 256 * g + 128 * u
            OP("sync", lambda e, r=r0, u_=u, gg=g: e.dma_start(
                out=x1B[gg % 2][:, 1024 * u_:1024 * u_ + 1024], in_=x1s[r:r + 128, :]), sX1B, 16)
        x1_ld[g] = sX1B.n
        W("sync", sX1B, x1_ld[g])

    for h_ in range(2):
        for w_ in range(3):
            issue_w(w_, h_)
    emit_loads(0)
    emit_loads(1)
    for g in range(NBLK):

        def emit_down(hc):
            bk2 = hc % 2
            W("tensor", sHT, ht_at_rec[hc])
            W("tensor", sCastB, castB[(2, hc)])
            if hc == 0:
                W("tensor", sEpB, ep_cnt[g - 1] if g > 0 else 0)
            dn_ = None
            for u in range(2):
                for hh in range(2):
                    fn = lambda e, h=hc, u_=u, h_=hh, b=bk2: e.matmul(
                        pDn[2 * u_ + h_][:, 0:512], lhsT=hT_s[b][:, 128 * u_:128 * u_ + 128],
                        rhs=Wd_s[:, 1024 * h + 512 * h_:1024 * h + 512 * h_ + 512],
                        start=h == 0, stop=h == HC - 1)
                    dn_ = OP("tensor", fn, sDnT, 1) if (u == 1 and hh == 1) else OP("tensor", fn)
            return dn_

        ht_at_rec = {}
        dn_at_rec = {}
        for hc in range(HC):
            bk = hc % 2
            if g == 0:
                if hc + 2 < HC:
                    for w_ in range(3):
                        issue_w(w_, hc + 2)
                for w_ in range(3):
                    cast_w(w_, hc)
            W("tensor", sCastB, castB[(0, hc)])
            W("tensor", sXnB, xn_ld[g])
            W("tensor", sSil, sil_at[bk])
            for dc in range(8):
                fn = lambda e, d=dc, h=hc, b=bk, gg=g: e.matmul(
                    pG[b][:, 0:256], lhsT=Wg_s[:, 1024 * h + 128 * d:1024 * h + 128 * d + 128],
                    rhs=xnfmB[gg % 2][:, 256 * d:256 * d + 256], start=d == 0, stop=d == 7)
                gq = OP("tensor", fn, sG, 1) if dc == 7 else OP("tensor", fn)
            W("scalar", sG, gq)
            W("scalar", sHT, ht_at[bk])
            sil_at[bk] = OP("scalar", lambda e, b=bk: e.activation(
                out=gsilu[b][:, :], in_=pG[b][:, 0:256], func=AF.Silu), sSil, 1)
            W("tensor", sCastB, castB[(1, hc)])
            for dc in range(8):
                fn = lambda e, d=dc, h=hc, b=bk, gg=g: e.matmul(
                    pU[b][:, 0:256], lhsT=Wu_s[:, 1024 * h + 128 * d:1024 * h + 128 * d + 128],
                    rhs=xnfmB[gg % 2][:, 256 * d:256 * d + 256], start=d == 0, stop=d == 7)
                uq = OP("tensor", fn, sU, 1) if dc == 7 else OP("tensor", fn)
            up_cnt[(g, hc)] = uq
            if hc >= 1:
                dn = emit_down(hc - 1)
                dn_at_rec[hc - 1] = dn
            W("vector", sU, uq)
            W("vector", sSil, sil_at[bk])
            if hc >= 2:
                W("vector", sDnT, dn_at_rec[hc - 2])
            ht_at[bk] = OP("vector", lambda e, b=bk: e.tensor_tensor(
                out=hT_s[b][:, :], in0=pU[b][:, 0:256], in1=gsilu[b][:, :], op=ALU.mult), sHT, 1)
            ht_at_rec[hc] = ht_at[bk]
            if hc == 4 and (g - 1) in ln3_pend:
                emit_ln3_tail(g - 1)
        dn = emit_down(HC - 1)

        W("vector", sDnT, dn)
        W("vector", sVc, y3_read[g - 1] if g > 0 else 0)
        ev3u = [0, 0]
        W("scalar", sDnT, dn)
        W("scalar", sVc, y3_read[g - 1] if g > 0 else 0)
        for u in range(2):
            eng = "vector" if u == 0 else "scalar"
            for hh in range(2):
                if eng == "vector":
                    ev3u[u] = OP(eng, lambda e, u_=u, h_=hh: e.tensor_copy(
                        y3B[u_][:, 512 * h_:512 * h_ + 512], pDn[2 * u_ + h_][:, 0:512]), sEpB, 1)
                else:
                    ev3u[u] = OP(eng, lambda e, u_=u, h_=hh: e.copy(
                        y3B[u_][:, 512 * h_:512 * h_ + 512], pDn[2 * u_ + h_][:, 0:512]), sEpB, 1)
        pw_d = [0, 0]
        for u in range(2):
            W("vector", sEpB, ev3u[u])
            W("vector", sMV, sMV.n)
            OP("vector", lambda e, u_=u: e.bn_stats(statsB[:, 0:6], y3B[u_][:, 0:512]), sBnG, 1)
            gg2 = OP("vector", lambda e, u_=u: e.bn_stats(statsB[:, 6:12], y3B[u_][:, 512:1024]), sBnG, 1)
            W("vector", sBnG, gg2)
            mvc = OP("vector", lambda e, u_=u: e.bn_aggr(mvB[:, 2 * u_:2 * u_ + 2], statsB[:, :]), sMV, 1)
            W("vector", sMV, mvc)
            vp = OP("vector", lambda e, u_=u: e.tensor_scalar(
                out=vpe_s[:, u_:u_ + 1], in0=mvB[:, 2 * u_ + 1:2 * u_ + 2], scalar1=EPS, scalar2=None,
                op0=ALU.add), sMV, 1)
            W("gpsimd", sMV, vp)
            W("gpsimd", sPw, 1)
            pw_d[u] = OP("gpsimd", lambda e, u_=u: e.tensor_tensor(
                out=istdB[:, u_:u_ + 1], in0=vpe_s[:, u_:u_ + 1], in1=nhalf[:, :], op=ALU.pow), sPw, 1)
        ln3_pend[g] = (pw_d[0], pw_d[1])
        ep_cnt[g] = sEpB.n
        if g + 2 < NBLK:
            emit_loads(g + 2)

    if (NBLK - 1) in ln3_pend:
        emit_ln3_tail(NBLK - 1)
    W("gpsimd", sOut, sOut.n)

    block = es.enter_context(nc.Block())
    block.sync(qs["sync"].run)
    block.tensor(qs["tensor"].run)
    block.vector(qs["vector"].run)
    block.scalar(qs["scalar"].run)
    block.gpsimd(qs["gpsimd"].run)
    esB.close()
    es.close()
    return nc


def _host_inputs(x, wqkv, wout, bout, wg, wu, wd):
    x = np.ascontiguousarray(np.asarray(x), dtype=np.float32)
    wqkvT = np.ascontiguousarray(np.asarray(wqkv, dtype=np.float32).T)
    woutT = np.ascontiguousarray(np.asarray(wout, dtype=np.float32).T)
    boutb = np.ascontiguousarray(np.broadcast_to(np.asarray(bout, dtype=np.float32)[None, :], (128, D)))
    wg = np.asarray(wg, dtype=np.int32)
    wu = np.asarray(wu, dtype=np.int32)
    wd = np.asarray(wd, dtype=np.int32)
    wgT = np.ascontiguousarray(wg.reshape(HC, 128, 8, 128).transpose(0, 3, 2, 1)).reshape(HC, 128, 1024)
    wuT = np.ascontiguousarray(wu.reshape(HC, 128, 8, 128).transpose(0, 3, 2, 1)).reshape(HC, 128, 1024)
    wdT = np.ascontiguousarray(wd.T).reshape(HC, 128, 1024)
    kk = np.arange(128)[:, None]
    qq = np.arange(256)[None, :]
    mask2 = np.where(kk <= qq, 0.0, NEG).astype(BF)
    mask3 = np.where(kk + 128 <= qq, 0.0, NEG).astype(BF)
    ident = np.eye(128, dtype=BF)
    onesc = np.ones((128, 1), dtype=BF)
    onesr = np.full((1, 128), -1.0, dtype=np.float32)
    onef = np.ones((1, 1), dtype=np.float32)
    econst = np.ascontiguousarray(
        np.broadcast_to(np.array([EPS, LNSCALE], dtype=np.float32)[None, :], (128, 2)))
    shared = dict(wqkvT=wqkvT, woutT=woutT, boutb=boutb, wgT=wgT, wuT=wuT, wdT=wdT,
                  mask2d=mask2, mask3d=mask3, identd=ident, onescd=onesc,
                  onesrd=onesr, onefd=onef, econstd=econst)
    in_maps = []
    for c in range(8):
        b, p = c // 2, c % 2
        if p == 1:
            xb = x[b]
        else:
            xb = x[b].reshape(NBLK, 2, 256, D)[:, ::-1].reshape(8192, D)
        m = dict(shared)
        m["xc"] = np.ascontiguousarray(xb)
        m["pc01d"] = np.full((128, 1), NEG if p == 0 else 0.0, dtype=np.float32)
        in_maps.append(m)
    return in_maps


def kernel(L0_attn_x, L0_attn_w_qkv, L0_attn_w_out, L0_attn_b_out,
           L0_ffn_w_gate, L0_ffn_w_up, L0_ffn_w_down):
    in_maps = _host_inputs(L0_attn_x, L0_attn_w_qkv, L0_attn_w_out, L0_attn_b_out,
                           L0_ffn_w_gate, L0_ffn_w_up, L0_ffn_w_down)
    nc = build_nc()
    res = run_bass_kernel_spmd(nc, in_maps, core_ids=list(range(8)))
    out = np.empty((4, 8192, D), dtype=np.float32)
    for c in range(8):
        b, p = c // 2, c % 2
        r = np.asarray(res.results[c]["out_c"])
        out[b].reshape(NBLK, 2, 256, D)[:, p] = r.reshape(NBLK, 256, D)
    return out


if __name__ == "__main__":
    import reference as _r
    ins = _r.setup_inputs()
    out = kernel(**{k: np.asarray(v) for k, v in ins.items()})
    ref = np.asarray(_r.reference(**ins))
    err = float(np.sqrt(((out - ref) ** 2).mean() / (ref ** 2).mean()))
    print("rel l2:", err)


# version 30
# speedup vs baseline: 1.0770x; 1.0345x over previous
import numpy as np
import ml_dtypes
import concourse.bass as bass
import concourse.mybir as mybir
from concourse.bass_utils import run_bass_kernel_spmd
from contextlib import ExitStack

F32 = mybir.dt.float32
BF16 = mybir.dt.bfloat16
I32 = mybir.dt.int32
AF = mybir.ActivationFunctionType
ALU = mybir.AluOpType
AX = mybir.AxisListType

NBLK = 16
D = 1024
HC = 24
SCALE = 512.0 ** -0.5
LNSCALE = float(np.log(SCALE * 1.25))
EPS = 1e-5
NEG = -1e30
QDEQ = 1.0 / 127.0
BF = ml_dtypes.bfloat16


class Sem:
    def __init__(self, es, nc, name):
        self.h = es.enter_context(nc.semaphore(name))
        self.n = 0


class EngQ:
    def __init__(self):
        self.ops = []

    def run(self, eng):
        for op in self.ops:
            op(eng)


def build_nc():
    nc = bass.Bass("TRN2", target_bir_lowering=False, debug=False)

    xc = nc.dram_tensor("xc", [8192, D], F32, kind="ExternalInput").ap()
    wqkvT = nc.dram_tensor("wqkvT", [D, 1536], F32, kind="ExternalInput").ap()
    woutT = nc.dram_tensor("woutT", [512, D], F32, kind="ExternalInput").ap()
    boutb = nc.dram_tensor("boutb", [128, D], F32, kind="ExternalInput").ap()
    wgT = nc.dram_tensor("wgT", [HC, 128, 1024], I32, kind="ExternalInput").ap()
    wuT = nc.dram_tensor("wuT", [HC, 128, 1024], I32, kind="ExternalInput").ap()
    wdT = nc.dram_tensor("wdT", [HC, 128, 1024], I32, kind="ExternalInput").ap()
    mask2d = nc.dram_tensor("mask2d", [128, 256], BF16, kind="ExternalInput").ap()
    mask3d = nc.dram_tensor("mask3d", [128, 256], BF16, kind="ExternalInput").ap()
    pc01d = nc.dram_tensor("pc01d", [128, 1], F32, kind="ExternalInput").ap()
    identd = nc.dram_tensor("identd", [128, 128], BF16, kind="ExternalInput").ap()
    onescd = nc.dram_tensor("onescd", [128, 1], BF16, kind="ExternalInput").ap()
    onesrd = nc.dram_tensor("onesrd", [1, 128], F32, kind="ExternalInput").ap()
    onefd = nc.dram_tensor("onefd", [1, 1], F32, kind="ExternalInput").ap()
    econstd = nc.dram_tensor("econstd", [128, 2], F32, kind="ExternalInput").ap()
    out_c = nc.dram_tensor("out_c", [4096, D], F32, kind="ExternalOutput").ap()
    x1s = nc.dram_tensor("x1s", [4096, D], F32, kind="Internal").ap()
    xnfms = nc.dram_tensor("xnfms", [8, 128, 4096], BF16, kind="Internal").ap()

    es = ExitStack()
    S = lambda name: Sem(es, nc, name)
    sX = S("sX"); sXr = S("sXr"); sW = S("sW"); sCast = S("sCast")
    sLN = S("sLN"); sTP = S("sTP"); sFM = S("sFM")
    sQKV = S("sQKV"); sEv = S("sEv")
    sBnG = S("sBnG"); sMV = S("sMV"); sIstd = S("sIstd")
    sBs = S("sBs"); sBt = S("sBt"); sBv = S("sBv"); sBq = S("sBq")
    sS = S("sS"); sMsk = S("sMsk"); sPT = S("sPT"); sPV = S("sPV")
    sLrow = S("sLrow"); sLt = S("sLt"); sRc = S("sRc")
    sOT = S("sOT"); sOP = S("sOP"); sY = S("sY"); sX1 = S("sX1"); sSp = S("sSp")
    sTd = S("sTd"); sVd = S("sVd"); sSd = S("sSd")
    sWB = S("sWB"); sCastB = S("sCastB"); sXnB = S("sXnB"); sX1B = S("sX1B")
    sG = S("sG"); sSil = S("sSil"); sU = S("sU"); sHT = S("sHT")
    sDnT = S("sDnT"); sPw = S("sPw"); sEpB = S("sEpB"); sX1Bt = S("sX1Bt")
    sOut = S("sOut"); sSc = S("sSc"); sVc = S("sVc"); sGq = S("sGq"); sWB2 = S("sWB2")

    pO01 = es.enter_context(nc.psum_tensor("pO01", [128, 512], F32))
    pO23 = es.enter_context(nc.psum_tensor("pO23", [128, 512], F32))
    pAux2 = es.enter_context(nc.psum_tensor("pAux2", [128, 512], F32))
    pAux3 = es.enter_context(nc.psum_tensor("pAux3", [128, 512], F32))
    pS0 = es.enter_context(nc.psum_tensor("pS0", [128, 512], F32))
    pS1 = es.enter_context(nc.psum_tensor("pS1", [128, 512], F32))

    qs = {k: EngQ() for k in ("sync", "tensor", "vector", "scalar", "gpsimd")}

    def W(eng, sem, val):
        if val > 0:
            qs[eng].ops.append(lambda e, s=sem.h, v=val: e.wait_ge(s, v))

    def OP(eng, fn, sem=None, k=1):
        if sem is None:
            qs[eng].ops.append(fn)
            return None
        sem.n += k
        qs[eng].ops.append(lambda e, f=fn, s=sem.h, kk=k: f(e).then_inc(s, kk))
        return sem.n

    esA = ExitStack()
    sb = lambda name, shape, dt: esA.enter_context(nc.sbuf_tensor(name, shape, dt))
    K_s = [sb(f"K{c}", [128, 8192], BF16) for c in range(4)]
    V_s = sb("V", [128, 64 * 512], BF16)
    Wq_s = sb("Wq", [128, 8 * 1536], BF16)
    Wo_s = sb("Wo", [128, 4 * 1024], BF16)
    bout_s = sb("bout", [128, 1024], BF16)
    mask2_s = sb("m2", [128, 256], BF16)
    mask3_s = sb("m3", [128, 256], BF16)
    pc01_s = sb("pc01", [128, 1], F32)
    ident_s = sb("ident", [128, 128], BF16)
    onesc_s = sb("onesc", [128, 1], BF16)
    onesr_s = sb("onesr", [1, 128], F32)
    onef_s = sb("onef", [1, 1], F32)
    econst_s = sb("econst", [128, 2], F32)
    xrot = [sb(f"xr{i}", [128, 1024], F32) for i in range(2)]
    xres_s = sb("xres", [128, 1024], F32)
    xn_s = [sb(f"xn{i}", [128, 1024], BF16) for i in range(2)]
    junk_s = xn_s[0]
    xnfm_s = sb("xnfm", [128, 8 * 256], BF16)
    Qfm_s = sb("Qfm", [128, 4 * 256], BF16)
    PT_s = [sb(f"PT{i}", [128, 256], BF16) for i in range(2)]
    OT_s = sb("OT", [128, 4 * 256], BF16)
    linv_s = sb("linv", [128, 2], F32)
    rows_s = sb("rows", [1, 520], F32)
    bias_s = sb("bias", [128, 4], F32)
    stats_s = sb("stats", [128, 12], F32)
    mv_s = sb("mv", [128, 8], F32)
    istd_s = sb("istd", [128, 8], F32)
    y2_s = sb("y2", [128, 2048], BF16)
    x1o_s = sb("x1o", [128, 1024], F32)
    xnb1_tm = [sb(f"xb1t{i}", [128, 1024], BF16) for i in range(2)]
    xnb1_s = sb("xb1s", [128, 8 * 256], BF16)
    pT = [esA.enter_context(nc.psum_tensor(f"pT{i}", [128, 1024], BF16)) for i in range(2)]

    for dst, src in ((mask2_s, mask2d), (mask3_s, mask3d), (pc01_s, pc01d),
                     (ident_s, identd), (onesc_s, onescd), (onesr_s, onesrd),
                     (onef_s, onefd), (econst_s, econstd)):
        OP("sync", lambda e, d=dst, s=src: e.dma_start(out=d[:, :], in_=s), sW, 16)
    OP("sync", lambda e: e.dma_start(out=x1o_s[:, :], in_=boutb), sW, 16)
    W("scalar", sW, sW.n)
    OP("scalar", lambda e: e.copy(bout_s[:, :], x1o_s[:, :]), sCast, 1)
    stage = [xrot[0], x1o_s]
    si = 0
    for dc in range(8):
        for (c0, w_) in ((0, 1024), (1024, 512)):
            buf = stage[si % 2]
            W("sync", sCast, sCast.n)
            OP("sync", lambda e, b=buf, d=dc, c=c0, w=w_: e.dma_start(
                out=b[:, 0:w], in_=wqkvT[128 * d:128 * d + 128, c:c + w]), sW, 16)
            W("sync", sW, sW.n)
            W("scalar", sW, sW.n)
            OP("scalar", lambda e, b=buf, d=dc, c=c0, w=w_: e.copy(
                Wq_s[:, 1536 * d + c:1536 * d + c + w], b[:, 0:w]), sCast, 1)
            si += 1
    for vc in range(4):
        buf = stage[si % 2]
        W("sync", sCast, sCast.n)
        OP("sync", lambda e, b=buf, v=vc: e.dma_start(
            out=b[:, :], in_=woutT[128 * v:128 * v + 128, :]), sW, 16)
        W("sync", sW, sW.n)
        W("scalar", sW, sW.n)
        OP("scalar", lambda e, b=buf, v=vc: e.copy(
            Wo_s[:, 1024 * v:1024 * v + 1024], b[:, :]), sCast, 1)
        si += 1
    castA_done = sCast.n
    OP("vector", lambda e: e.memset(rows_s[0:1, 514:515], 0.0), sBv, 1)

    ln_cnt = {}
    x_cnt = {}
    expc = {}
    pvc = {}
    spill_after = [0] * (NBLK + 1)
    x1tt = {}
    fm_evac_at = {0: 0, 1: 0}
    ev_at = {0: 0, 1: 0}
    sy_step = [0] * (NBLK + 1)
    st = {"flash_s_prev": 0, "sq_done_prev": 0, "prev_half_qkv": 0}
    tp_xn_at = {0: 0, 1: 0}
    xb1tp_at = {0: 0, 1: 0}
    nb_d = {}
    gp1_d = {}
    gpk_d = {}
    xn0r_d = {}
    xn1r_d = {}
    gpk23_d = {}
    bm_st = {}
    qmx_d = {}
    bn_d = {}
    x1sp = {}
    xnsp = {}
    yfree_d = {}
    istd2_d = {}
    istdb_d = {}
    xr_d = {}
    qkv_ev_d = {}
    bq_d = {}
    flash_done_d = {}
    rc_d = {}

    def stage_LN1(j, subs=(0, 1, 2, 3)):
        for s in subs:
            b = s % 2
            c0 = 2 * b
            prev = ln_cnt.get((j, s - 2), ln_cnt.get((j - 1, s + 2), 0))
            if prev > 0:
                W("sync", sLN, prev)
            if j == 0:
                W("sync", sCast, castA_done)
            r0 = 512 * j + 128 * s
            x_cnt[(j, s)] = OP("sync", lambda e, r=r0, xb=xrot[b]: e.dma_start(
                out=xb[:, :], in_=xc[r:r + 128, :]), sX, 16)
            W("sync", sX, x_cnt[(j, s)])
            W("vector", sX, x_cnt[(j, s)])
            W("vector", sMV, sMV.n)
            OP("vector", lambda e, xb=xrot[b]: e.bn_stats(stats_s[:, 0:6], xb[:, 0:512]), sBnG, 1)
            g = OP("vector", lambda e, xb=xrot[b]: e.bn_stats(stats_s[:, 6:12], xb[:, 512:1024]), sBnG, 1)
            W("vector", sBnG, g)
            mvc = OP("vector", lambda e, c=c0: e.bn_aggr(mv_s[:, c:c + 2], stats_s[:, :]), sMV, 1)
            W("scalar", sMV, mvc)
            lnch = OP("scalar", lambda e, c=c0: e.activation(
                out=istd_s[:, c + 1:c + 2], in_=mv_s[:, c + 1:c + 2],
                func=AF.Ln, bias=econst_s[:, 0:1], scale=1.0), sSc, 1)
            W("scalar", sSc, lnch)
            ic = OP("scalar", lambda e, c=c0: e.activation(
                out=istd_s[:, c:c + 1], in_=istd_s[:, c + 1:c + 2],
                func=AF.Exp, scale=-0.5), sIstd, 1)
            W("vector", sIstd, ic)
            if s == 0 and j > 0:
                W("vector", sGq, gp1_d.get(j - 1, 0))
                W("vector", sBt, xn0r_d.get(j - 1, 0))
            if s == 1 and j > 0:
                W("vector", sGq, gpk_d.get(j - 1, 0))
                W("vector", sBt, xn1r_d.get(j - 1, 0))
            W("vector", sTP, tp_xn_at[b])
            ln_cnt[(j, s)] = OP("vector", lambda e, xb=xrot[b], c=c0, o=xn_s[b]: e.tensor_scalar(
                out=o[:, :], in0=xb[:, :], scalar1=mv_s[:, c:c + 1], scalar2=istd_s[:, c:c + 1],
                op0=ALU.subtract, op1=ALU.mult), sLN, 1)

    def stage_qkv(j):
        for s in range(4):
            b = s % 2
            if s == 2:
                stage_LN1(j, (2, 3))
            W("tensor", sLN, ln_cnt[(j, s)])
            W("tensor", sFM, fm_evac_at[b])
            for c in range(8):
                fn = lambda e, cc=c, xb=xn_s[b], pb=pT[b]: e.transpose(
                    pb[:, 128 * cc:128 * cc + 128], xb[:, 128 * cc:128 * cc + 128], ident_s[:, :])
                tpc = OP("tensor", fn, sTP, 1) if c == 7 else OP("tensor", fn)
            tp_xn_at[b] = tpc
            W("scalar", sTP, tpc)
            if j == 0 and s == 0:
                W("scalar", sCast, castA_done)
            W("scalar", sQKV, st["prev_half_qkv"])
            u = s % 2
            src = pT[b].rearrange("p (c w) -> p c w", w=128)
            dst = xnfm_s.rearrange("p (c w) -> p c w", w=256)[:, :, 128 * u:128 * u + 128]
            fm_evac_at[b] = OP("scalar", lambda e, d=dst, sr=src: e.copy(d, sr), sFM, 1)
            drain_epi(1)

            if s in (1, 3):
                h = s // 2
                W("tensor", sFM, fm_evac_at[0])
                W("tensor", sFM, fm_evac_at[1])
                if j == 0 and h == 0:
                    W("tensor", sCast, castA_done)
                if h == 0:
                    W("tensor", sY, sy_step[j - 1] if j > 0 else 0)
                groups = [("K", kc) for kc in range(4)] + [("V", v) for v in range(2)]
                if h == 1:
                    groups += [("Q", qc) for qc in range(4)]
                for gi, (kind, idx) in enumerate(groups):
                    bank = (pS0, pS1)[gi % 2]
                    W("tensor", sEv, ev_at[gi % 2])
                    for dc in range(8):
                        if kind == "K":
                            fn = lambda e, d=dc, k=idx, bk=bank: e.matmul(
                                bk[:, 0:256],
                                lhsT=Wq_s[:, 1536 * d + 512 + 128 * k:1536 * d + 512 + 128 * k + 128],
                                rhs=xnfm_s[:, 256 * d:256 * d + 256], start=d == 0, stop=d == 7)
                        elif kind == "V":
                            fn = lambda e, d=dc, v=idx, bk=bank: e.matmul(
                                bk[:, 0:512],
                                lhsT=xnfm_s[:, 256 * d + 128 * v:256 * d + 128 * v + 128],
                                rhs=Wq_s[:, 1536 * d + 1024:1536 * d + 1536], start=d == 0, stop=d == 7)
                        else:
                            fn = lambda e, d=dc, q=idx, bk=bank: e.matmul(
                                bk[:, 0:256],
                                lhsT=Wq_s[:, 1536 * d + 128 * q:1536 * d + 128 * q + 128],
                                rhs=xnfm_s[:, 256 * d:256 * d + 256], start=d == 0, stop=d == 7)
                        qk = OP("tensor", fn, sQKV, 1) if dc == 7 else OP("tensor", fn)
                    W("scalar", sQKV, qk)
                    if kind == "K":
                        col = 512 * j + 256 * h
                        evc = OP("scalar", lambda e, k=idx, cl=col, bk=bank: e.copy(
                            K_s[k][:, cl:cl + 256], bk[:, 0:256]), sEv, 1)
                    elif kind == "V":
                        t = 4 * j + 2 * h + idx
                        evc = OP("scalar", lambda e, t_=t, bk=bank: e.copy(
                            V_s[:, 512 * t_:512 * t_ + 512], bk[:, 0:512]), sEv, 1)
                    else:
                        if idx == 0:
                            W("scalar", sS, st["flash_s_prev"])
                            W("scalar", sGq, gp1_d.get(j - 1, 0))
                        evc = OP("scalar", lambda e, q=idx, bk=bank: e.copy(
                            Qfm_s[:, 256 * q:256 * q + 256], bk[:, 0:256]), sEv, 1)
                    ev_at[gi % 2] = evc
                st["prev_half_qkv"] = sQKV.n
        qkv_ev_d[j] = sEv.n

    def stage_bmach_gp(j):
        W("gpsimd", sTP, tp_xn_at[0])
        W("gpsimd", sLN, ln_cnt[(j, 2)])
        W("gpsimd", sEv, qkv_ev_d[j])
        gp1_d[j] = OP("gpsimd", lambda e: e.tensor_tensor(
            out=junk_s[:, :], in0=Qfm_s[:, :], in1=Qfm_s[:, :], op=ALU.mult), sGq, 1)
        W("gpsimd", sTP, tp_xn_at[1])
        W("gpsimd", sLN, ln_cnt[(j, 3)])
        col = 512 * j
        for i in range(2):
            gpk = OP("gpsimd", lambda e, i_=i, cl=col: e.tensor_tensor(
                out=xn_s[1][:, 512 * i_:512 * i_ + 512], in0=K_s[i_][:, cl:cl + 512],
                in1=K_s[i_][:, cl:cl + 512], op=ALU.mult), sGq, 1)
        gpk_d[j] = gpk
        W("gpsimd", sOP, op_d.get((j - 1, 1, 1), 0))
        W("gpsimd", sOT, 2 * j)
        for i in range(2):
            gpk23 = OP("gpsimd", lambda e, i_=i, cl=col: e.tensor_tensor(
                out=OT_s[:, 512 * i_:512 * i_ + 512], in0=K_s[2 + i_][:, cl:cl + 512],
                in1=K_s[2 + i_][:, cl:cl + 512], op=ALU.mult), sGq, 1)
        gpk23_d[j] = gpk23

    def stage_bmach_mm1(j):
        W("tensor", sGq, gp1_d[j])
        if j > 0:
            W("tensor", sLrow, j + 1)
        for c in range(4):
            qmm = OP("tensor", lambda e, cc=c: e.matmul(
                pAux2[0:1, 0:256], lhsT=onesc_s[:, :],
                rhs=junk_s[:, 256 * cc:256 * cc + 256],
                start=cc == 0, stop=cc == 3), sBt, 1)
        xn0r_d[j] = qmm
        W("tensor", sGq, gpk_d[j])
        W("tensor", sBs, bn_d.get(j - 1, 0))
        for i in range(2):
            OP("tensor", lambda e, i_=i: e.matmul(
                pAux3[0:1, 0:512], lhsT=onesc_s[:, :],
                rhs=xn_s[1][:, 512 * i_:512 * i_ + 512],
                start=i_ == 0, stop=False), sBt, 1)
        W("tensor", sGq, gpk23_d[j])
        for i in range(2):
            kmm = OP("tensor", lambda e, i_=i: e.matmul(
                pAux3[0:1, 0:512], lhsT=onesc_s[:, :],
                rhs=OT_s[:, 512 * i_:512 * i_ + 512],
                start=False, stop=i_ == 1), sBt, 1)
        xn1r_d[j] = kmm
        W("vector", sBt, qmm)
        qmx = OP("vector", lambda e: e.reduce_max(rows_s[0:1, 512:513], pAux2[0:1, 0:256],
                                                  axis=AX.X), sBv, 1)
        qmx_d[j] = qmx
        W("vector", sBt, kmm)
        kb = OP("vector", lambda e: e.reduce_max(rows_s[0:1, 513:514], pAux3[0:1, 0:512],
                                                 axis=AX.X), sBv, 1)
        W("vector", sBv, kb)
        kmx = OP("vector", lambda e: e.tensor_tensor(out=rows_s[0:1, 514:515], in0=rows_s[0:1, 513:514],
                                                     in1=rows_s[0:1, 514:515], op=ALU.max), sBv, 1)
        W("vector", sBv, kmx)
        bv = OP("vector", lambda e: e.tensor_tensor(out=rows_s[0:1, 515:516], in0=rows_s[0:1, 512:513],
                                                    in1=rows_s[0:1, 514:515], op=ALU.mult), sBv, 1)
        bm_st[j] = bv

    def stage_bmach_mm2(j):
        bv = bm_st[j]
        W("scalar", sBv, bv)
        lnch = OP("scalar", lambda e: e.activation(out=rows_s[0:1, 516:517], in_=rows_s[0:1, 515:516],
                                                   func=AF.Ln, scale=1.0), sSc, 1)
        W("scalar", sSc, lnch)
        nb_d[j] = OP("scalar", lambda e: e.activation(out=rows_s[0:1, 517:518], in_=rows_s[0:1, 516:517],
                                                      func=AF.Exp, bias=econst_s[0:1, 1:2], scale=0.5), sBs, 1)

    def stage_bmachB2(j):
        W("tensor", sBs, nb_d[j])
        bc = OP("tensor", lambda e: e.matmul(pAux3[:, 0:1], lhsT=onesr_s[0:1, :],
                                             rhs=rows_s[0:1, 517:518], start=True, stop=True), sBt, 1)
        W("scalar", sBt, bc)
        bsl = 2 * (j % 2)
        bn = OP("scalar", lambda e, b0=bsl: e.copy(bias_s[:, b0:b0 + 1], pAux3[:, 0:1]), sBs, 1)
        bn_d[j] = bn
        W("vector", sBs, bn)
        bq_d[j] = OP("vector", lambda e, b0=bsl: e.tensor_tensor(out=bias_s[:, b0 + 1:b0 + 2],
                                                                 in0=bias_s[:, b0:b0 + 1],
                                                                 in1=pc01_s[:, :], op=ALU.add), sBq, 1)

    def stage_flash(j):
        ntile = 4 * j + 4
        W("scalar", sBq, max(j, 1))

        def emit_S(t):
            bank = (pS0, pS1)[t % 2]
            if t >= 2:
                W("tensor", sPT, expc[(j, t - 2)])
            else:
                W("tensor", sEv, qkv_ev_d[j])
            for c in range(4):
                fn = lambda e, cc=c, t_=t, bk=bank: e.matmul(
                    bk[:, 0:256], lhsT=K_s[cc][:, 128 * t_:128 * t_ + 128],
                    rhs=Qfm_s[:, 256 * cc:256 * cc + 256], start=cc == 0, stop=cc == 3)
                sg = OP("tensor", fn, sS, 1) if c == 3 else OP("tensor", fn)
            diag = t >= 4 * j + 2
            if diag:
                W("vector", sS, sg)
                msk = mask2_s if t == 4 * j + 2 else mask3_s
                mk = OP("vector", lambda e, bk=bank, m=msk: e.tensor_tensor(
                    out=bk[:, 0:256], in0=bk[:, 0:256], in1=m[:, :], op=ALU.add), sMsk, 1)
                W("scalar", sMsk, mk)
            else:
                W("scalar", sS, sg)
            if t >= 2:
                W("scalar", sPV, pvc[(j, t - 2)])
            bp = 2 * (max(j - 1, 0) % 2)
            bias_ap = bias_s[:, bp + 1:bp + 2] if (4 * j <= t < 4 * j + 2) else bias_s[:, bp:bp + 1]
            expc[(j, t)] = OP("scalar", lambda e, bk=bank, ba=bias_ap, pt=PT_s[t % 2]: e.activation(
                out=pt[:, :], in_=bk[:, 0:256], func=AF.Exp, bias=ba, scale=SCALE), sPT, 1)

        def emit_PV(t):
            W("tensor", sPT, expc[(j, t)])
            if t == 0:
                W("tensor", sY, sy_step[j - 1] if j > 0 else 0)
                W("tensor", sLrow, j)
                W("tensor", sBv, qmx_d.get(j - 1, qmx_d.get(0, 0)))
                W("tensor", sRc, rc_d.get(j - 1, 0))
            first = t == 0
            lastt = t == ntile - 1
            for c in range(4):
                pb = pO01 if c < 2 else pO23
                cc0 = 256 * (c % 2)
                stf = first and (c % 2 == 0)
                OP("tensor", lambda e, t_=t, cc=c, pb_=pb, c0=cc0, st_=stf, sp_=lastt: e.matmul(
                    pb_[:, c0:c0 + 256], lhsT=V_s[:, 512 * t_ + 128 * cc:512 * t_ + 128 * cc + 128],
                    rhs=PT_s[t_ % 2][:, :], start=st_, stop=sp_, skip_group_check=True))
            pvc[(j, t)] = OP("tensor", lambda e, t_=t, st_=first, sp_=lastt: e.matmul(
                pAux2[0:1, 0:256], lhsT=onesc_s[:, :], rhs=PT_s[t_ % 2][:, :],
                start=st_, stop=sp_), sPV, 1)

        for t in range(ntile):
            emit_S(t)
            if t >= 1:
                emit_PV(t - 1)
        emit_PV(ntile - 1)
        st["flash_s_prev"] = sS.n
        flash_done_d[j] = pvc[(j, ntile - 1)]

    def stage_out(j):
        ntile = 4 * j + 4
        flash_done = flash_done_d[j]
        W("scalar", sPV, flash_done)
        W("scalar", sLt, 2 * j)
        lr = OP("scalar", lambda e: e.copy(rows_s[0:1, 0:256], pAux2[0:1, 0:256]), sLrow, 1)
        W("tensor", sLrow, lr)
        W("tensor", sBv, qmx_d.get(j, 0))
        W("tensor", sRc, rc_d.get(j - 1, 0))
        for u in range(2):
            lt = OP("tensor", lambda e, u_=u: e.matmul(
                pAux2[:, 256 + u_:257 + u_], lhsT=rows_s[0:1, 128 * u_:128 * u_ + 128],
                rhs=onef_s[0:1, 0:1], start=True, stop=True), sLt, 1)
        W("vector", sLt, lt)
        rc_d[j] = OP("vector", lambda e: e.reciprocal(linv_s[:, 0:2], pAux2[:, 256:258]), sRc, 1)
        W("scalar", sBt, xn1r_d.get(j, 0))
        OP("scalar", lambda e: e.copy(OT_s[:, 0:512], pO01[:, :]), sOT, 1)
        W("vector", sPV, flash_done)
        W("vector", sBt, xn1r_d.get(j, 0))
        ot = OP("vector", lambda e: e.tensor_copy(OT_s[:, 512:1024], pO23[:, :]), sOT, 1)
        for u in range(2):
            for hh in range(2):
                pb = (pO01, pO23)[hh] if u == 0 else (pS0, pS1)[hh]
                W("tensor", sOT, ot)
                if u == 1 and hh == 0:
                    W("tensor", sPT, expc[(j, ntile - 1)])
                for c in range(4):
                    fn = lambda e, cc=c, u_=u, h_=hh, pb_=pb: e.matmul(
                        pb_[:, 0:512], lhsT=OT_s[:, 256 * cc + 128 * u_:256 * cc + 128 * u_ + 128],
                        rhs=Wo_s[:, 1024 * cc + 512 * h_:1024 * cc + 512 * h_ + 512],
                        start=cc == 0, stop=cc == 3)
                    opc = OP("tensor", fn, sOP, 1) if c == 3 else OP("tensor", fn)
                op_d[(j, u, hh)] = opc

    def stage_epi_stt(j):
        for u in range(2):
            for hh in range(2):
                pb = (pO01, pO23)[hh] if u == 0 else (pS0, pS1)[hh]
                W("vector", sOP, op_d[(j, u, hh)])
                if u == 0 and hh == 0:
                    W("vector", sRc, rc_d[j])
                    if j == 0:
                        W("vector", sCast, castA_done)
                    W("vector", sLN, yfree_d.get(j - 1, 0))
                OP("vector", lambda e, u_=u, h_=hh, pb_=pb: e.scalar_tensor_tensor(
                    out=y2_s[:, 1024 * u_ + 512 * h_:1024 * u_ + 512 * h_ + 512], in0=pb_[:, 0:512],
                    scalar=linv_s[:, u_:u_ + 1], in1=bout_s[:, 512 * h_:512 * h_ + 512],
                    op0=ALU.mult, op1=ALU.add), sY, 1)
        sy_step[j] = sY.n

    epi_pend = []

    def drain_epi(n):
        k = 0
        while epi_pend and k < n:
            epi_pend.pop(0)()
            k += 1

    def stage_epi_prepare(j):
        def ln2_piece(u):
            def f():
                W("vector", sY, sy_step[j])
                W("vector", sMV, sMV.n)
                OP("vector", lambda e, u_=u: e.bn_stats(stats_s[:, 0:6], y2_s[:, 1024 * u_:1024 * u_ + 512]), sBnG, 1)
                g = OP("vector", lambda e, u_=u: e.bn_stats(stats_s[:, 6:12], y2_s[:, 1024 * u_ + 512:1024 * u_ + 1024]), sBnG, 1)
                W("vector", sBnG, g)
                mvc = OP("vector", lambda e: e.bn_aggr(mv_s[:, 4:6], stats_s[:, :]), sMV, 1)
                W("scalar", sMV, mvc)
                lnch = OP("scalar", lambda e: e.activation(out=istd_s[:, 5:6], in_=mv_s[:, 5:6],
                                                    func=AF.Ln, bias=econst_s[:, 0:1], scale=1.0), sSc, 1)
                W("scalar", sSc, lnch)
                ic = OP("scalar", lambda e: e.activation(out=istd_s[:, 4:5], in_=istd_s[:, 5:6],
                                                         func=AF.Exp, scale=-0.5), sIstd, 1)
                istd2_d[(j, u)] = ic
            return f

        def x1_piece(u):
            def f():
                W("vector", sIstd, istd2_d[(j, u)])
                prevsp = x1sp.get((j, u - 1), x1sp.get((j - 1, 1), 0))
                W("vector", sSp, prevsp)
                vch = OP("vector", lambda e, u_=u: e.tensor_scalar(
                    out=x1o_s[:, :], in0=y2_s[:, 1024 * u_:1024 * u_ + 1024], scalar1=mv_s[:, 4:5],
                    scalar2=istd_s[:, 4:5], op0=ALU.subtract, op1=ALU.mult), sVc, 1)
                W("vector", sVc, vch)
                W("vector", sXr, xr_d[(j, u)])
                x1tt[(j, u)] = OP("vector", lambda e: e.tensor_tensor(
                    out=x1o_s[:, :], in0=x1o_s[:, :], in1=xres_s[:, :], op=ALU.add), sX1, 1)
                W("vector", sX1, x1tt[(j, u)])
                W("vector", sMV, sMV.n)
                OP("vector", lambda e: e.bn_stats(stats_s[:, 0:6], x1o_s[:, 0:512]), sBnG, 1)
                g = OP("vector", lambda e: e.bn_stats(stats_s[:, 6:12], x1o_s[:, 512:1024]), sBnG, 1)
                W("vector", sBnG, g)
                mvc = OP("vector", lambda e: e.bn_aggr(mv_s[:, 6:8], stats_s[:, :]), sMV, 1)
                W("scalar", sMV, mvc)
                lnch = OP("scalar", lambda e: e.activation(out=istd_s[:, 7:8], in_=mv_s[:, 7:8],
                                                    func=AF.Ln, bias=econst_s[:, 0:1], scale=1.0), sSc, 1)
                W("scalar", sSc, lnch)
                ic = OP("scalar", lambda e: e.activation(out=istd_s[:, 6:7], in_=istd_s[:, 7:8],
                                                         func=AF.Exp, scale=-0.5), sIstd, 1)
                istdb_d[(j, u)] = ic
            return f

        def norm_piece(u):
            def f():
                W("vector", sIstd, istdb_d[(j, u)])
                W("vector", sTP, xb1tp_at[u])
                ln_cnt[(j, 4 + u)] = OP("vector", lambda e, u_=u: e.tensor_scalar(
                    out=xnb1_tm[u_][:, :], in0=x1o_s[:, :], scalar1=mv_s[:, 6:7], scalar2=istd_s[:, 6:7],
                    op0=ALU.subtract, op1=ALU.mult), sLN, 1)
                if u == 1:
                    yfree_d[j] = ln_cnt[(j, 5)]
                W("gpsimd", sX1, x1tt[(j, u)])
                r0 = 256 * j + 128 * u
                x1sp[(j, u)] = OP("gpsimd", lambda e, r=r0: e.dma_start(out=x1s[r:r + 128, :], in_=x1o_s[:, :]), sSp, 16)
                W("gpsimd", sSp, sSp.n)
                if u == 0:
                    r1 = 512 * j + 256 + 128
                elif j + 1 < NBLK:
                    r1 = 512 * (j + 1) + 256
                else:
                    r1 = None
                if r1 is not None:
                    OP("gpsimd", lambda e, r=r1: e.dma_start(out=xres_s[:, :], in_=xc[r:r + 128, :]), sXr, 16)
                    W("gpsimd", sXr, sXr.n)
                    xr_d[(j + u, 1 - u)] = sXr.n
            return f

        for u in range(2):
            epi_pend.append(ln2_piece(u))
            epi_pend.append(x1_piece(u))
            epi_pend.append(norm_piece(u))

    def stage_xnb1(j):
        for u in range(2):
            b = u % 2
            W("tensor", sLN, ln_cnt[(j, 4 + u)])
            W("tensor", sFM, fm_evac_at[b])
            for c in range(8):
                fn = lambda e, cc=c, u_=u, pb=pT[b]: e.transpose(
                    pb[:, 128 * cc:128 * cc + 128], xnb1_tm[u_][:, 128 * cc:128 * cc + 128], ident_s[:, :])
                tpb = OP("tensor", fn, sTP, 1) if c == 7 else OP("tensor", fn)
            xb1tp_at[u] = tpb
            W("scalar", sTP, tpb)
            W("scalar", sSp, xnsp.get(j - 1, 0))
            src = pT[b].rearrange("p (c w) -> p c w", w=128)
            dst = xnb1_s.rearrange("p (c w) -> p c w", w=256)[:, :, 128 * u:128 * u + 128]
            fm_evac_at[b] = OP("scalar", lambda e, d=dst, sr=src: e.copy(d, sr), sFM, 1)
        W("gpsimd", sFM, fm_evac_at[1])
        for c in range(8):
            OP("gpsimd", lambda e, cc=c, jj=j: e.dma_start(
                out=xnfms[cc, :, 256 * jj:256 * jj + 256], in_=xnb1_s[:, 256 * cc:256 * cc + 256]), sSp, 16)
        xnsp[j] = sSp.n
        W("gpsimd", sSp, xnsp[j])

    op_d = {}
    OP("gpsimd", lambda e: e.dma_start(out=xres_s[:, :], in_=xc[256:384, :]), sXr, 16)
    W("gpsimd", sXr, sXr.n)
    xr_d[(0, 0)] = sXr.n
    stage_LN1(0, (0, 1))
    stage_qkv(0)
    drain_epi(99)
    stage_bmach_gp(0)
    stage_bmach_mm1(0)
    stage_bmach_mm2(0)
    stage_bmachB2(0)
    for j in range(NBLK):
        if j >= 2:
            stage_bmachB2(j - 1)
        if 1 <= j < NBLK - 1:
            stage_bmach_gp(j)
        stage_flash(j)
        if 1 <= j < NBLK - 1:
            stage_bmach_mm1(j)
        stage_out(j)
        if 1 <= j < NBLK - 1:
            stage_bmach_mm2(j)
        if j + 1 < NBLK:
            stage_LN1(j + 1, (0, 1))
        stage_epi_stt(j)
        stage_epi_prepare(j)
        if j >= 1:
            stage_xnb1(j - 1)
        if j + 1 < NBLK:
            stage_qkv(j + 1)
        drain_epi(99)
    stage_xnb1(NBLK - 1)

    tD = OP("tensor", lambda e: e.nop(), sTd, 1)
    vD = OP("vector", lambda e: e.nop(), sVd, 1)
    scD = OP("scalar", lambda e: e.nop(), sSd, 1)
    esA.close()

    esB = ExitStack()
    sbB = lambda name, shape, dt: esB.enter_context(nc.sbuf_tensor(name, shape, dt))
    Wg_s = sbB("Wg", [128, HC * 1024], BF16)
    Wu_s = sbB("Wu", [128, HC * 1024], BF16)
    Wd_s = sbB("Wd", [128, HC * 1024], BF16)
    wstgB = [sbB(f"wstg{i}", [128, 1024], I32) for i in range(4)]
    xnfmB = [sbB(f"xnB{i}", [128, 8 * 256], BF16) for i in range(2)]
    x1B = [sbB(f"x1B{i}", [128, 2048], F32) for i in range(2)]
    gsilu = [sbB(f"gs{i}", [128, 256], BF16) for i in range(2)]
    hT_s = [sbB(f"hT{i}", [128, 256], BF16) for i in range(2)]
    outB = [sbB(f"oB{i}", [128, 1024], F32) for i in range(2)]
    y3B = [sbB(f"y3B{i}", [128, 1024], F32) for i in range(2)]
    statsB = sbB("stB", [128, 12], F32)
    mvB = sbB("mvB", [128, 4], F32)
    vpe_s = sbB("vpe", [128, 2], F32)
    istdB = sbB("isB", [128, 2], F32)
    nhalf = sbB("nh", [128, 1], F32)
    pB6 = esB.enter_context(nc.psum_tensor("pB6", [128, 512], F32))
    pB7 = esB.enter_context(nc.psum_tensor("pB7", [128, 512], F32))
    pDn = [pO01, pO23, pAux2, pAux3]
    pG = [pS0, pB6]
    pU = [pS1, pB7]

    _all_sems = [sX, sXr, sW, sCast, sLN, sTP, sFM, sQKV, sEv, sBnG, sMV, sIstd,
                 sBs, sBt, sBv, sBq, sS, sMsk, sPT, sPV, sLrow, sLt, sRc, sOT,
                 sOP, sY, sX1, sSp, sSc, sVc, sGq, sTd, sVd, sSd]
    for e_ in ("sync", "gpsimd", "scalar", "vector", "tensor"):
        for _sm in _all_sems:
            W(e_, _sm, _sm.n)
    W("sync", sSp, xnsp[NBLK - 1])
    OP("gpsimd", lambda e: e.memset(nhalf[:, :], -0.5), sPw, 1)

    castB = {}
    wi = [0]

    wiC = [0]

    def issue_w(w, hc):
        k = wi[0]
        ring = "sync" if k % 2 == 0 else "gpsimd"
        sem = sWB if k % 2 == 0 else sWB2
        buf = wstgB[k % 4]
        W(ring, sCastB, max(0, k - 3))
        W(ring, sem, 16 * (k // 2))
        src = (wgT, wuT, wdT)[w]
        OP(ring, lambda e, b=buf, s=src, h=hc: e.dma_start(out=b[:, :], in_=s[h, :, :]), sem, 16)
        wi[0] += 1

    def cast_w(w, hc):
        k = wiC[0]
        buf = wstgB[k % 4]
        W("scalar", sWB if k % 2 == 0 else sWB2, 16 * (k // 2 + 1))
        dstw = (Wg_s, Wu_s, Wd_s)[w]
        castB[(w, hc)] = OP("scalar", lambda e, b=buf, d=dstw, h=hc: e.activation(
            out=d[:, 1024 * h:1024 * h + 1024], in_=b[:, :], func=AF.Copy, bias=0.0, scale=QDEQ),
            sCastB, 1)
        wiC[0] += 1

    xn_ld = [0] * NBLK
    x1_ld = [0] * NBLK
    y3_read = {}
    ln3_pend = {}
    ln3s_pend = {}

    def emit_ln3_tail(g):
        pw0, pw1 = ln3_pend.pop(g)
        for u in range(2):
            W("vector", sPw, (pw0, pw1)[u])
            W("vector", sOut, st_cnt.get((g - 1, u), 0))
            for hh in range(2):
                y3v = OP("vector", lambda e, u_=u, h_=hh: e.tensor_scalar(
                    out=outB[u_][:, 512 * h_:512 * h_ + 512], in0=y3B[u_][:, 512 * h_:512 * h_ + 512],
                    scalar1=mvB[:, 2 * u_:2 * u_ + 1], scalar2=istdB[:, u_:u_ + 1],
                    op0=ALU.subtract, op1=ALU.mult), sVc, 1)
            y3_read[g] = y3v
            W("vector", sVc, y3v)
            W("vector", sX1B, x1_ld[g])
            tt = OP("vector", lambda e, u_=u, gg=g: e.tensor_tensor(
                out=outB[u_][:, :], in0=outB[u_][:, :], in1=x1B[gg % 2][:, 1024 * u_:1024 * u_ + 1024],
                op=ALU.add), sX1Bt, 1)
            W("gpsimd", sX1Bt, tt)
            r0 = 256 * g + 128 * u
            st_cnt[(g, u)] = OP("gpsimd", lambda e, r=r0, u_=u: e.dma_start(
                out=out_c[r:r + 128, :], in_=outB[u_][:, :]), sOut, 16)
            W("gpsimd", sOut, st_cnt[(g, u)])
    up_cnt = {}
    sil_at = {0: 0, 1: 0}
    ht_at = {0: 0, 1: 0}
    ep_cnt = [0] * (NBLK + 1)
    st_cnt = {}

    def emit_loads(g):
        if g >= 2:
            W("sync", sU, up_cnt[(g - 2, HC - 1)])
        for c in range(8):
            OP("sync", lambda e, cc=c, gg=g: e.dma_start(
                out=xnfmB[gg % 2][:, 256 * cc:256 * cc + 256],
                in_=xnfms[cc, :, 256 * gg:256 * gg + 256]), sXnB, 16)
        xn_ld[g] = sXnB.n
        W("sync", sXnB, xn_ld[g])
        W("sync", sSp, xnsp[g])
        if g >= 2:
            W("sync", sX1Bt, 2 * (g - 1))
        for u in range(2):
            r0 = 256 * g + 128 * u
            OP("sync", lambda e, r=r0, u_=u, gg=g: e.dma_start(
                out=x1B[gg % 2][:, 1024 * u_:1024 * u_ + 1024], in_=x1s[r:r + 128, :]), sX1B, 16)
        x1_ld[g] = sX1B.n
        W("sync", sX1B, x1_ld[g])

    for h_ in range(2):
        for w_ in range(3):
            issue_w(w_, h_)
    emit_loads(0)
    emit_loads(1)
    for g in range(NBLK):

        def emit_down(hc):
            bk2 = hc % 2
            W("tensor", sHT, ht_at_rec[hc])
            W("tensor", sCastB, castB[(2, hc)])
            if hc == 0:
                W("tensor", sEpB, ep_cnt[g - 1] if g > 0 else 0)
            dn_ = None
            for u in range(2):
                for hh in range(2):
                    fn = lambda e, h=hc, u_=u, h_=hh, b=bk2: e.matmul(
                        pDn[2 * u_ + h_][:, 0:512], lhsT=hT_s[b][:, 128 * u_:128 * u_ + 128],
                        rhs=Wd_s[:, 1024 * h + 512 * h_:1024 * h + 512 * h_ + 512],
                        start=h == 0, stop=h == HC - 1)
                    dn_ = OP("tensor", fn, sDnT, 1) if (u == 1 and hh == 1) else OP("tensor", fn)
            return dn_

        ht_at_rec = {}
        dn_at_rec = {}
        for hc in range(HC):
            bk = hc % 2
            if g == 0:
                if hc + 2 < HC:
                    for w_ in range(3):
                        issue_w(w_, hc + 2)
                for w_ in range(3):
                    cast_w(w_, hc)
            W("tensor", sCastB, castB[(0, hc)])
            W("tensor", sXnB, xn_ld[g])
            W("tensor", sSil, sil_at[bk])
            for dc in range(8):
                fn = lambda e, d=dc, h=hc, b=bk, gg=g: e.matmul(
                    pG[b][:, 0:256], lhsT=Wg_s[:, 1024 * h + 128 * d:1024 * h + 128 * d + 128],
                    rhs=xnfmB[gg % 2][:, 256 * d:256 * d + 256], start=d == 0, stop=d == 7)
                gq = OP("tensor", fn, sG, 1) if dc == 7 else OP("tensor", fn)
            W("scalar", sG, gq)
            W("scalar", sHT, ht_at[bk])
            sil_at[bk] = OP("scalar", lambda e, b=bk: e.activation(
                out=gsilu[b][:, :], in_=pG[b][:, 0:256], func=AF.Silu), sSil, 1)
            W("tensor", sCastB, castB[(1, hc)])
            for dc in range(8):
                fn = lambda e, d=dc, h=hc, b=bk, gg=g: e.matmul(
                    pU[b][:, 0:256], lhsT=Wu_s[:, 1024 * h + 128 * d:1024 * h + 128 * d + 128],
                    rhs=xnfmB[gg % 2][:, 256 * d:256 * d + 256], start=d == 0, stop=d == 7)
                uq = OP("tensor", fn, sU, 1) if dc == 7 else OP("tensor", fn)
            up_cnt[(g, hc)] = uq
            if hc >= 1:
                dn = emit_down(hc - 1)
                dn_at_rec[hc - 1] = dn
            W("vector", sU, uq)
            W("vector", sSil, sil_at[bk])
            if hc >= 2:
                W("vector", sDnT, dn_at_rec[hc - 2])
            ht_at[bk] = OP("vector", lambda e, b=bk: e.tensor_tensor(
                out=hT_s[b][:, :], in0=pU[b][:, 0:256], in1=gsilu[b][:, :], op=ALU.mult), sHT, 1)
            ht_at_rec[hc] = ht_at[bk]
            if hc == 2 and (g - 1) in ln3s_pend:
                ln3s_pend.pop(g - 1)()
            if hc == 4 and (g - 1) in ln3_pend:
                emit_ln3_tail(g - 1)
        dn = emit_down(HC - 1)

        W("vector", sDnT, dn)
        W("vector", sVc, y3_read[g - 1] if g > 0 else 0)
        ev3u = [0, 0]
        W("scalar", sDnT, dn)
        W("scalar", sVc, y3_read[g - 1] if g > 0 else 0)
        for u in range(2):
            eng = "vector" if u == 0 else "scalar"
            for hh in range(2):
                if eng == "vector":
                    ev3u[u] = OP(eng, lambda e, u_=u, h_=hh: e.tensor_copy(
                        y3B[u_][:, 512 * h_:512 * h_ + 512], pDn[2 * u_ + h_][:, 0:512]), sEpB, 1)
                else:
                    ev3u[u] = OP(eng, lambda e, u_=u, h_=hh: e.copy(
                        y3B[u_][:, 512 * h_:512 * h_ + 512], pDn[2 * u_ + h_][:, 0:512]), sEpB, 1)
        def mk_ln3_stats(gg, e0, e1):
            def f():
                pw_d = [0, 0]
                for u in range(2):
                    W("vector", sEpB, (e0, e1)[u])
                    W("vector", sMV, sMV.n)
                    OP("vector", lambda e, u_=u: e.bn_stats(statsB[:, 0:6], y3B[u_][:, 0:512]), sBnG, 1)
                    gg2 = OP("vector", lambda e, u_=u: e.bn_stats(statsB[:, 6:12], y3B[u_][:, 512:1024]), sBnG, 1)
                    W("vector", sBnG, gg2)
                    mvc = OP("vector", lambda e, u_=u: e.bn_aggr(mvB[:, 2 * u_:2 * u_ + 2], statsB[:, :]), sMV, 1)
                    W("vector", sMV, mvc)
                    vp = OP("vector", lambda e, u_=u: e.tensor_scalar(
                        out=vpe_s[:, u_:u_ + 1], in0=mvB[:, 2 * u_ + 1:2 * u_ + 2], scalar1=EPS, scalar2=None,
                        op0=ALU.add), sMV, 1)
                    W("gpsimd", sMV, vp)
                    W("gpsimd", sPw, 1)
                    pw_d[u] = OP("gpsimd", lambda e, u_=u: e.tensor_tensor(
                        out=istdB[:, u_:u_ + 1], in0=vpe_s[:, u_:u_ + 1], in1=nhalf[:, :], op=ALU.pow), sPw, 1)
                ln3_pend[gg] = (pw_d[0], pw_d[1])
            return f

        ln3s_pend[g] = mk_ln3_stats(g, ev3u[0], ev3u[1])
        ep_cnt[g] = sEpB.n
        if g + 2 < NBLK:
            emit_loads(g + 2)

    if (NBLK - 1) in ln3s_pend:
        ln3s_pend.pop(NBLK - 1)()
    if (NBLK - 1) in ln3_pend:
        emit_ln3_tail(NBLK - 1)
    W("gpsimd", sOut, sOut.n)

    block = es.enter_context(nc.Block())
    block.sync(qs["sync"].run)
    block.tensor(qs["tensor"].run)
    block.vector(qs["vector"].run)
    block.scalar(qs["scalar"].run)
    block.gpsimd(qs["gpsimd"].run)
    esB.close()
    es.close()
    return nc


def _host_inputs(x, wqkv, wout, bout, wg, wu, wd):
    x = np.ascontiguousarray(np.asarray(x), dtype=np.float32)
    wqkvT = np.ascontiguousarray(np.asarray(wqkv, dtype=np.float32).T)
    woutT = np.ascontiguousarray(np.asarray(wout, dtype=np.float32).T)
    boutb = np.ascontiguousarray(np.broadcast_to(np.asarray(bout, dtype=np.float32)[None, :], (128, D)))
    wg = np.asarray(wg, dtype=np.int32)
    wu = np.asarray(wu, dtype=np.int32)
    wd = np.asarray(wd, dtype=np.int32)
    wgT = np.ascontiguousarray(wg.reshape(HC, 128, 8, 128).transpose(0, 3, 2, 1)).reshape(HC, 128, 1024)
    wuT = np.ascontiguousarray(wu.reshape(HC, 128, 8, 128).transpose(0, 3, 2, 1)).reshape(HC, 128, 1024)
    wdT = np.ascontiguousarray(wd.T).reshape(HC, 128, 1024)
    kk = np.arange(128)[:, None]
    qq = np.arange(256)[None, :]
    mask2 = np.where(kk <= qq, 0.0, NEG).astype(BF)
    mask3 = np.where(kk + 128 <= qq, 0.0, NEG).astype(BF)
    ident = np.eye(128, dtype=BF)
    onesc = np.ones((128, 1), dtype=BF)
    onesr = np.full((1, 128), -1.0, dtype=np.float32)
    onef = np.ones((1, 1), dtype=np.float32)
    econst = np.ascontiguousarray(
        np.broadcast_to(np.array([EPS, LNSCALE], dtype=np.float32)[None, :], (128, 2)))
    shared = dict(wqkvT=wqkvT, woutT=woutT, boutb=boutb, wgT=wgT, wuT=wuT, wdT=wdT,
                  mask2d=mask2, mask3d=mask3, identd=ident, onescd=onesc,
                  onesrd=onesr, onefd=onef, econstd=econst)
    in_maps = []
    for c in range(8):
        b, p = c // 2, c % 2
        if p == 1:
            xb = x[b]
        else:
            xb = x[b].reshape(NBLK, 2, 256, D)[:, ::-1].reshape(8192, D)
        m = dict(shared)
        m["xc"] = np.ascontiguousarray(xb)
        m["pc01d"] = np.full((128, 1), NEG if p == 0 else 0.0, dtype=np.float32)
        in_maps.append(m)
    return in_maps


def kernel(L0_attn_x, L0_attn_w_qkv, L0_attn_w_out, L0_attn_b_out,
           L0_ffn_w_gate, L0_ffn_w_up, L0_ffn_w_down):
    in_maps = _host_inputs(L0_attn_x, L0_attn_w_qkv, L0_attn_w_out, L0_attn_b_out,
                           L0_ffn_w_gate, L0_ffn_w_up, L0_ffn_w_down)
    nc = build_nc()
    res = run_bass_kernel_spmd(nc, in_maps, core_ids=list(range(8)))
    out = np.empty((4, 8192, D), dtype=np.float32)
    for c in range(8):
        b, p = c // 2, c % 2
        r = np.asarray(res.results[c]["out_c"])
        out[b].reshape(NBLK, 2, 256, D)[:, p] = r.reshape(NBLK, 256, D)
    return out


if __name__ == "__main__":
    import reference as _r
    ins = _r.setup_inputs()
    out = kernel(**{k: np.asarray(v) for k, v in ins.items()})
    ref = np.asarray(_r.reference(**ins))
    err = float(np.sqrt(((out - ref) ** 2).mean() / (ref ** 2).mean()))
    print("rel l2:", err)
